# version 17 (final)
# baseline (speedup 1.0000x reference)
import sys
sys.path.insert(0, "/opt/trn_rl_repo")
import contextlib
import numpy as np
import concourse.bass as bass
import concourse.mybir as mybir
from concourse.bass_utils import run_bass_kernel_spmd

F32 = mybir.dt.float32
BF16 = mybir.dt.bfloat16
I32 = mybir.dt.int32
AF = mybir.ActivationFunctionType
ALU = mybir.AluOpType
AX = mybir.AxisListType

T = 1024
D = 2048
II = 8192
NH = 16
TB = 8
DC = 16
IB = 64
EB = 16
SC = 0.08838834764831845
EPS = 1e-5


def build_nc():
    nc = bass.Bass("TRN2", target_bir_lowering=False, debug=False)

    x_d = nc.dram_tensor("x", [T, D], F32, kind="ExternalInput")
    wq_d = nc.dram_tensor("wq", [EB, 128, D], F32, kind="ExternalInput")
    wk_d = nc.dram_tensor("wk", [EB, 128, D], F32, kind="ExternalInput")
    wv_d = nc.dram_tensor("wv", [EB, 128, D], F32, kind="ExternalInput")
    wo_d = nc.dram_tensor("wo", [EB, 128, D], F32, kind="ExternalInput")
    bq_d = nc.dram_tensor("bq", [D], F32, kind="ExternalInput")
    bk_d = nc.dram_tensor("bk", [D], F32, kind="ExternalInput")
    bv_d = nc.dram_tensor("bv", [D], F32, kind="ExternalInput")
    bo_d = nc.dram_tensor("bo", [D], F32, kind="ExternalInput")
    wg_d = nc.dram_tensor("wg", [IB, 128, D], I32, kind="ExternalInput")
    wu_d = nc.dram_tensor("wu", [IB, 128, D], I32, kind="ExternalInput")
    wd_d = nc.dram_tensor("wd", [EB, 128, II], I32, kind="ExternalInput")
    gs_d = nc.dram_tensor("gs", [II], F32, kind="ExternalInput")
    us_d = nc.dram_tensor("us", [II], F32, kind="ExternalInput")
    ds_d = nc.dram_tensor("dsc", [D], F32, kind="ExternalInput")
    id_d = nc.dram_tensor("ident", [128, 128], F32, kind="ExternalInput")

    out_d = nc.dram_tensor("out", [T, D], F32, kind="ExternalOutput")

    qT_d = nc.dram_tensor("qTd", [D, T], BF16, kind="Internal")
    kvi_d = nc.dram_tensor("kv_in", [2 * D, T], BF16, kind="Internal")
    kvo_d = nc.dram_tensor("kv_out", [8 * D, T], BF16, kind="Internal")
    ffn_d = nc.dram_tensor("ffnd", [D, T], F32, kind="Internal")

    BASE = 17920
    off = [BASE]

    def sb(name, shape, dtype, at=None):
        nbytes = int(np.prod(shape[1:])) * (4 if dtype in (F32, I32) else 2)
        if at is None:
            at = off[0]
            off[0] = at + ((nbytes + 63) // 64) * 64
        return nc.alloc_sbuf_tensor_at(name, shape, dtype, offset=at)

    ident_f = sb("ident_f", [128, 128], F32)
    ident_b = sb("ident_b", [128, 128], BF16)
    ones128 = sb("ones128", [128, 128], BF16)
    negrow = sb("negrow", [1, 128], BF16)
    bqt = sb("bqt", [128, EB], F32)
    bkt = sb("bkt", [128, EB], F32)
    bvt = sb("bvt", [128, EB], F32)
    bot = sb("bot", [128, EB], F32)
    gst = sb("gst", [128, IB], F32)
    ust = sb("ust", [128, IB], F32)
    dst = sb("dst", [128, EB], F32)
    vrow_f = sb("vrow_f", [1, T], F32)
    vrow_b = sb("vrow_b", [1, T], BF16)
    stat0 = sb("stat0", [128, 8], F32)
    stat1 = sb("stat1", [128, 8], F32)
    sg0 = sb("sg0", [128, T], F32)
    sg1 = sb("sg1", [128, T], F32)
    _recip_at = off[0]
    recipb = sb("recipb", [128, 512], F32)
    rsb128 = sb("rsb128", [128, T], BF16, at=_recip_at)
    qh0 = sb("qh0", [128, T], BF16)
    qh1 = sb("qh1", [128, T], BF16)

    wbuf = sb("wbuf", [128, 12288], BF16)
    WA = [wbuf[:, 0:2048], wbuf[:, 2048:4096], wbuf[:, 4096:6144], wbuf[:, 6144:8192]]
    WD = [wbuf[:, 0:4096], wbuf[:, 4096:8192], wbuf[:, 8192:12288]]

    R0 = off[0]
    assert R0 + 163840 <= 229344, f"sbuf overflow R0={R0}"

    xnT = sb("xnT", [128, DC * T], BF16, at=R0)
    h_sb = sb("h_sb", [128, IB * T], BF16, at=R0 + 32768)
    bq16 = sb("bq16", [16, 128], F32, at=R0 + 36864)
    bk16 = sb("bk16", [16, 128], F32, at=R0 + 37376)
    bv16 = sb("bv16", [16, 128], F32, at=R0 + 37888)
    bo16 = sb("bo16", [16, 128], F32, at=R0 + 38400)
    gs64 = sb("gs64", [64, 128], F32, at=R0 + 38912)
    us64 = sb("us64", [64, 128], F32, at=R0 + 39424)
    ds16 = sb("ds16", [16, 128], F32, at=R0 + 39936)
    wvres = sb("wvres", [128, DC * 2048], BF16, at=R0 + 65536)
    xsq0 = sb("xsq0", [128, D], F32, at=R0 + 40960)
    bv_row = sb("bv_row", [1, D], BF16, at=R0 + 40960)
    xsq1 = sb("xsq1", [128, D], F32, at=R0 + 49152)
    xn_t0 = sb("xn_t0", [128, D], BF16, at=R0 + 57344)
    xn_t1 = sb("xn_t1", [128, D], BF16, at=R0 + 61440)
    stg0 = sb("stg0", [128, T], BF16, at=R0 + 32768)
    stg1 = sb("stg1", [128, T], BF16, at=R0 + 34816)
    kh0 = sb("kh0", [128, 4096], BF16, at=R0 + 0)
    kh1 = sb("kh1", [128, 4096], BF16, at=R0 + 8192)
    vh0 = sb("vh0", [128, 4096], BF16, at=R0 + 16384)
    vh1 = sb("vh1", [128, 4096], BF16, at=R0 + 24576)
    oT = sb("oT", [128, NH * T], BF16, at=R0 + 32768)
    pT0 = sb("pT0", [128, 32 * 512], BF16, at=R0 + 65536)
    pT1 = sb("pT1", [128, 32 * 512], BF16, at=R0 + 98304)
    fti0 = sb("fti0", [128, T], F32, at=R0 + 0)
    fti1 = sb("fti1", [128, T], F32, at=R0 + 4096)
    ote0 = sb("ote0", [128, T], F32, at=R0 + 8192)
    ote1 = sb("ote1", [128, T], F32, at=R0 + 12288)
    OF = [sb(f"of{i}", [128, 128], F32, at=R0 + 16384 + 512 * i) for i in range(8)]
    xf = sb("xf", [128, TB * D], F32, at=R0 + 98304)
    xf1 = sb("xf1", [128, 8 * T], F32, at=R0 + 65536)
    xf2 = sb("xf2", [128, 8 * T], F32, at=R0 + 98304)
    EXG = [sb(f"exg{k}", [128, 2048], BF16, at=R0 + 98304 + k * 8192) for k in range(8)]
    EXU = [sb(f"exu{k}", [128, 2048], BF16, at=R0 + 98304 + k * 8192 + 4096) for k in range(8)]
    EXG2 = [sb(f"exg2_{k}", [128, 2048], BF16, at=R0 + 65536 + k * 8192) for k in range(4)]
    EXU2 = [sb(f"exu2_{k}", [128, 2048], BF16, at=R0 + 65536 + k * 8192 + 4096) for k in range(4)]

    es = contextlib.ExitStack()
    sem_names = (["cl", "xl", "xl0", "id", "wv", "e0", "e1", "e2", "e3", "e4", "e5", "e6", "e7", "f0", "f1", "f2", "f3", "w0", "w1", "w2", "st0", "st1", "qw0", "qw1",
                  "cc", "kh0", "kh1", "vh0", "vh1", "qh0", "qh1", "fw0", "fw1",
                  "fr0", "fr1", "xf", "xf2", "pe", "ac", "dv"]
                 + [f"ou{i}" for i in range(8)])
    SEM = {n: es.enter_context(nc.semaphore(n)) for n in sem_names}
    blk = es.enter_context(nc.Block())

    C = {n: 0 for n in sem_names}
    LASTW = {}

    def w8(ename, eng, s, val):
        if val <= 0:
            return
        if LASTW.get((ename, s), 0) >= val:
            return
        LASTW[(ename, s)] = val
        eng.wait_ge(SEM[s], val)

    @blk.sync
    def _(s):
        s.dma_start(out=xf[:, 0:2 * D].rearrange("p (n d) -> p n d", n=2),
                    in_=x_d.ap()[0:256, :].rearrange("(n p) d -> p n d", p=128)
                    ).then_inc(SEM["xl0"], 16)
        s.dma_start(out=ident_f[:, :], in_=id_d[:, :]).then_inc(SEM["id"], 16)
        for dst_t, src in ((bq16, bq_d), (bk16, bk_d), (bv16, bv_d), (bo16, bo_d),
                           (gs64, gs_d), (us64, us_d), (ds16, ds_d)):
            s.dma_start(out=dst_t[:, :], in_=src.ap().rearrange("(c p) -> c p", p=128)
                        ).then_inc(SEM["cl"], 16)
        s.dma_start(out=xf[:, 2 * D:].rearrange("p (n d) -> p n d", n=6),
                    in_=x_d.ap()[256:1024, :].rearrange("(n p) d -> p n d", p=128)
                    ).then_inc(SEM["xl"], 16)
    C["xl0"] = 16
    C["xl"] = 16
    C["id"] = 16
    C["cl"] = 112

    @blk.vector
    def _(v):
        w8("v", v, "id", 16)
        v.tensor_copy(ident_b[:, :], ident_f[:, :]).then_inc(SEM["dv"], 1)
        v.memset(ones128[:, :], 1.0).then_inc(SEM["dv"], 1)
        v.memset(negrow[:, :], -128.0).then_inc(SEM["dv"], 1)
    C["dv"] = 3

    p0src = [(bq16, 16, bqt, EB), (bk16, 16, bkt, EB), (bv16, 16, bvt, EB),
             (bo16, 16, bot, EB), (gs64, 64, gst, IB), (us64, 64, ust, IB),
             (ds16, 16, dst, EB)]

    with nc.psum_tensor("p_b0", [128, 64], F32) as p_b0, \
         nc.psum_tensor("p_b1", [128, 64], F32) as p_b1:
        p_b = [p_b0, p_b1]
        pe0, ac0, dv0 = C["pe"], C["ac"], C["dv"]

        @blk.tensor
        def _(t):
            w8("t", t, "cl", 112)
            w8("t", t, "id", 16)
            w8("t", t, "dv", 3)
            for k, (src_t, kk, dst_t, ncol) in enumerate(p0src):
                if k >= 2:
                    w8("t", t, "dv", dv0 + k - 1)
                t.transpose(p_b[k % 2][:, 0:kk], src_t[:, :], ident_f[0:kk, 0:kk]
                            ).then_inc(SEM["pe"], 1)

        @blk.vector
        def _(v):
            for k, (src_t, kk, dst_t, ncol) in enumerate(p0src):
                w8("v", v, "pe", pe0 + k + 1)
                v.tensor_copy(dst_t[:, 0:ncol], p_b[k % 2][:, 0:ncol]).then_inc(SEM["dv"], 1)
        C["pe"] += 7
        C["dv"] += 7
    dv_bias_end = C["dv"]

    xsq = [xsq0, xsq1]
    xn_t = [xn_t0, xn_t1]
    stat = [stat0, stat1]
    with nc.psum_tensor("p_x0", [128, 128], BF16) as p_x0, \
         nc.psum_tensor("p_x1", [128, 128], BF16) as p_x1:
        p_x = [p_x0, p_x1]
        pe0, ac0, dv0 = C["pe"], C["ac"], C["dv"]

        dv_tbl = {}
        ac_tbl = {}
        cnt = dv0
        for n in range(TB):
            for op in ("r1", "msq", "m", "mm", "var"):
                cnt += 1
                dv_tbl[(op, n)] = cnt
            if n >= 1:
                for dc in range(0, DC, 2):
                    cnt += 1
                    dv_tbl[("ev", n - 1, dc)] = cnt
            for op in ("rstd", "xn"):
                cnt += 1
                dv_tbl[(op, n)] = cnt
        for dc in range(0, DC, 2):
            cnt += 1
            dv_tbl[("ev", TB - 1, dc)] = cnt
        dv_end_p1 = cnt
        cnt = ac0
        for n in range(TB):
            cnt += 1
            ac_tbl[("sq", n)] = cnt
            if n >= 1:
                for dc in range(1, DC, 2):
                    cnt += 1
                    ac_tbl[("ev", n - 1, dc)] = cnt
            cnt += 1
            ac_tbl[("sd", n)] = cnt
        for dc in range(1, DC, 2):
            cnt += 1
            ac_tbl[("ev", TB - 1, dc)] = cnt
        ac_end_p1 = cnt
        pe_tr = lambda n, dc: pe0 + 16 * n + dc + 1

        @blk.scalar
        def _(sc):
            def evacs_a(n):
                for dc in range(1, DC, 2):
                    w8("s", sc, "pe", pe_tr(n, dc))
                    sc.activation(xnT[:, dc * T + n * 128: dc * T + (n + 1) * 128],
                                  p_x[1][:, :], AF.Identity).then_inc(SEM["ac"], 1)

            for n in range(TB):
                w8("s", sc, "xl0" if n < 2 else "xl", 16)
                if n >= 2:
                    w8("s", sc, "ac", ac_tbl[("sq", n - 2)])
                    w8("s", sc, "dv", dv_tbl[("msq", n - 2)])
                sc.activation(xsq[n % 2][:, :], xf[:, n * D:(n + 1) * D], AF.Square,
                              accum_out=stat[n % 2][:, 1:2]).then_inc(SEM["ac"], 1)
                if n >= 1:
                    evacs_a(n - 1)
                w8("s", sc, "dv", dv_tbl[("var", n)])
                sc.activation(stat[n % 2][:, 6:7], stat[n % 2][:, 5:6], AF.Sqrt
                              ).then_inc(SEM["ac"], 1)
            evacs_a(TB - 1)

        @blk.vector
        def _(v):
            def stats(n):
                st = stat[n % 2]
                w8("v", v, "xl0" if n < 2 else "xl", 16)
                v.reduce_sum(st[:, 0:1], xf[:, n * D:(n + 1) * D], axis=AX.X).then_inc(SEM["dv"], 1)
                w8("v", v, "ac", ac_tbl[("sq", n)])
                v.tensor_scalar_mul(st[:, 3:4], st[:, 1:2], 1.0 / D).then_inc(SEM["dv"], 1)
                w8("v", v, "dv", dv_tbl[("r1", n)])
                v.tensor_scalar_mul(st[:, 2:3], st[:, 0:1], 1.0 / D).then_inc(SEM["dv"], 1)
                w8("v", v, "dv", dv_tbl[("m", n)])
                v.tensor_mul(st[:, 4:5], st[:, 2:3], st[:, 2:3]).then_inc(SEM["dv"], 1)
                w8("v", v, "dv", dv_tbl[("mm", n)])
                v.tensor_scalar(out=st[:, 5:6], in0=st[:, 3:4], scalar1=st[:, 4:5],
                                scalar2=EPS, op0=ALU.subtract, op1=ALU.add
                                ).then_inc(SEM["dv"], 1)
                if n >= 1:
                    evacs(n - 1)
                w8("v", v, "ac", ac_tbl[("sd", n)])
                v.reciprocal(st[:, 7:8], st[:, 6:7]).then_inc(SEM["dv"], 1)
                w8("v", v, "dv", dv_tbl[("rstd", n)])
                v.tensor_scalar(out=xn_t[n % 2][:, :], in0=xf[:, n * D:(n + 1) * D],
                                scalar1=st[:, 2:3], scalar2=st[:, 7:8],
                                op0=ALU.subtract, op1=ALU.mult).then_inc(SEM["dv"], 1)

            def evacs(n):
                for dc in range(0, DC, 2):
                    w8("v", v, "pe", pe_tr(n, dc))
                    v.tensor_copy(xnT[:, dc * T + n * 128: dc * T + (n + 1) * 128],
                                  p_x[0][:, :]).then_inc(SEM["dv"], 1)

            for n in range(TB):
                stats(n)
            evacs(TB - 1)

        @blk.tensor
        def _(t):
            w8("t", t, "dv", dv_bias_end)
            for n in range(TB):
                w8("t", t, "dv", dv_tbl[("xn", n)])
                for dc in range(DC):
                    k = n * 16 + dc
                    if k >= 2:
                        pn, pdc = divmod(k - 2, 16)
                        if pdc % 2 == 0:
                            w8("t", t, "dv", dv_tbl[("ev", pn, pdc)])
                        else:
                            w8("t", t, "ac", ac_tbl[("ev", pn, pdc)])
                    t.transpose(p_x[dc % 2][:, :], xn_t[n % 2][:, dc * 128:(dc + 1) * 128],
                                ident_b[:, :]).then_inc(SEM["pe"], 1)

        C["pe"] += 128
        C["ac"] = ac_end_p1
        C["dv"] = dv_end_p1


    with nc.psum_tensor("p_rs", [1, T], F32) as p_rs:
        pe0, dv0 = C["pe"], C["dv"]
        ac0_p2 = C["ac"]

        @blk.tensor
        def _(t):
            w8("t", t, "dv", dv0)
            w8("t", t, "ac", ac0_p2)
            ins = None
            for dc in range(DC):
                for j in range(2):
                    ins = t.matmul(p_rs[:, j * 512:(j + 1) * 512], ones128[:, 0:1],
                                   xnT[:, dc * T + j * 512: dc * T + (j + 1) * 512],
                                   start=(dc == 0), stop=(dc == DC - 1))
            ins.then_inc(SEM["pe"], 1)

        with nc.psum_tensor("p_bc", [128, T], F32) as p_bc:
            @blk.vector
            def _(v):
                w8("v", v, "pe", pe0 + 1)
                v.tensor_copy(vrow_f[:, :], p_rs[:, :]).then_inc(SEM["dv"], 1)
                w8("v", v, "dv", dv0 + 1)
                v.tensor_copy(vrow_b[:, :], vrow_f[:, :]).then_inc(SEM["dv"], 1)
                w8("v", v, "pe", pe0 + 2)
                v.tensor_copy(rsb128[:, :], p_bc[:, :]).then_inc(SEM["dv"], 1)

            @blk.tensor
            def _(t):
                w8("t", t, "dv", dv0 + 2)
                ins = None
                for j in range(2):
                    ins = t.matmul(p_bc[:, j * 512:(j + 1) * 512], negrow[:, :],
                                   vrow_b[:, j * 512:(j + 1) * 512],
                                   start=True, stop=True)
                ins.then_inc(SEM["pe"], 1)
        C["pe"] += 2
        C["dv"] += 3

    wtile = [WA[0], WA[1]]
    stg = [stg0, stg1]
    GSLOT = [wbuf[:, s * 4096: s * 4096 + 2048] for s in range(3)]
    USLOT = [wbuf[:, s * 4096 + 2048: (s + 1) * 4096] for s in range(3)]
    gslot = lambda ib: (ib + 2) % 3
    w_tgt = {}

    pe0, ac0, dv0 = C["pe"], C["ac"], C["dv"]
    w_base = {0: C["w0"], 1: C["w1"]}
    st_base = {0: C["st0"], 1: C["st1"]}
    pe_k = lambda eb: pe0 + eb + 1
    pe_v = lambda tb: pe0 + 16 + tb + 1
    pe_q = lambda eb: pe0 + 24 + eb + 1
    ac_k = lambda eb: ac0 + eb + 1
    ac_v = lambda tb, eh: ac0 + 16 + 2 * tb + eh + 1
    ac_q = lambda eb: ac0 + 32 + eb + 1
    kvi_v = kvi_d.ap()[D:2 * D, :]

    @blk.gpsimd
    def _(g):
        for eb in range(EB):
            par = eb % 2
            if eb >= 2:
                w8("g", g, "pe", pe_k(eb - 2))
            g.dma_start(out=wtile[par][:, :], in_=wk_d.ap()[eb, :, :]
                        ).then_inc(SEM[f"w{par}"], 16)
            C[f"w{par}"] += 16
            g.dma_start(out=wvres[:, eb * 2048:(eb + 1) * 2048],
                        in_=wv_d.ap()[eb, :, :]).then_inc(SEM["wv"], 16)
            C["wv"] += 16
        w8("g", g, "dv", dv0)
        g.dma_start(out=bv_row[:, :], in_=bv_d.ap().rearrange("(a q) -> a q", a=1)
                    ).then_inc(SEM["wv"], 16)
        C["wv"] += 16
        for eb in range(EB):
            par = eb % 2
            if eb < 2:
                w8("g", g, "pe", pe_k(14 + eb))
            else:
                w8("g", g, "pe", pe_q(eb - 2))
            g.dma_start(out=wtile[par][:, :], in_=wq_d.ap()[eb, :, :]
                        ).then_inc(SEM[f"w{par}"], 16)
            C[f"w{par}"] += 16
        g.dma_start(out=GSLOT[2][:, :], in_=wg_d.ap()[0, :, :]).then_inc(SEM["w2"], 16)
        g.dma_start(out=USLOT[2][:, :], in_=wu_d.ap()[0, :, :]).then_inc(SEM["w2"], 16)
        C["w2"] += 32
        w_tgt[0] = C["w2"]
        for k in range(8):
            g.dma_start(out=EXG[k][:, :], in_=wg_d.ap()[k + 1, :, :]
                        ).then_inc(SEM[f"e{k}"], 16)
            g.dma_start(out=EXU[k][:, :], in_=wu_d.ap()[k + 1, :, :]
                        ).then_inc(SEM[f"e{k}"], 16)
            C[f"e{k}"] += 32

    with nc.psum_tensor("p_pr0", [128, T], F32) as p_pr0, \
         nc.psum_tensor("p_pr1", [128, T], F32) as p_pr1:
        p_pr = [p_pr0, p_pr1]

        @blk.tensor
        def _(t):
            w8("t", t, "dv", dv0)
            for eb in range(EB):
                par = eb % 2
                w8("t", t, f"w{par}", w_base[par] + 16 * (eb // 2 + 1))
                if eb >= 2:
                    w8("t", t, "ac", ac_k(eb - 2))
                ins = None
                for dc in range(DC):
                    for j in range(2):
                        ins = t.matmul(
                            p_pr[par][:, j * 512:(j + 1) * 512],
                            wtile[par][:, dc * 128:(dc + 1) * 128],
                            xnT[:, dc * T + j * 512: dc * T + (j + 1) * 512],
                            start=(dc == 0), stop=(dc == DC - 1))
                ins.then_inc(SEM["pe"], 1)

        @blk.scalar
        def _(sc):
            w8("s", sc, "dv", dv_bias_end)
            for eb in range(EB):
                par = eb % 2
                w8("s", sc, "pe", pe_k(eb))
                if eb >= 2:
                    w8("s", sc, f"st{par}", st_base[par] + 16 * (eb // 2))
                sc.activation(stg[par][:, :], p_pr[par][:, :], AF.Identity,
                              bias=bkt[:, eb:eb + 1]).then_inc(SEM["ac"], 1)

        @blk.sync
        def _(s):
            for eb in range(EB):
                par = eb % 2
                w8("sy", s, "ac", ac_k(eb))
                s.dma_start(out=kvi_d.ap()[eb * 128:(eb + 1) * 128, :],
                            in_=stg[par][:, :]).then_inc(SEM[f"st{par}"], 16)
                C[f"st{par}"] += 16

        with nc.psum_tensor("p_vA", [128, 2048], F32) as p_vA:
            def v_chunk(tb, ch):
                if tb % 2 == 0:
                    return p_vA[:, ch * 512:(ch + 1) * 512]
                if ch < 2:
                    return p_pr0[:, ch * 512:(ch + 1) * 512]
                return p_pr1[:, (ch - 2) * 512:(ch - 1) * 512]

            def v_half(tb, eh):
                if tb % 2 == 0:
                    return p_vA[:, eh * 1024:(eh + 1) * 1024]
                return (p_pr0 if eh == 0 else p_pr1)[:, :]

            @blk.tensor
            def _(t):
                w8("t", t, "wv", 16 * 17)
                for tb in range(TB):
                    if tb % 2 == 1:
                        w8("t", t, "ac", ac_k(EB - 1))
                    if tb >= 2:
                        w8("t", t, "ac", ac_v(tb - 2, 1))
                    ins = None
                    for dc in range(DC):
                        for ch in range(4):
                            ins = t.matmul(
                                v_chunk(tb, ch),
                                xnT[:, dc * T + tb * 128: dc * T + (tb + 1) * 128],
                                wvres[:, dc * 2048 + ch * 512: dc * 2048 + (ch + 1) * 512],
                                start=(dc == 0), stop=False)
                    for ch in range(4):
                        ins = t.matmul(v_chunk(tb, ch), ones128[0:1, :],
                                       bv_row[:, ch * 512:(ch + 1) * 512],
                                       start=False, stop=True)
                    ins.then_inc(SEM["pe"], 1)

            @blk.scalar
            def _(sc):
                for tb in range(TB):
                    w8("s", sc, "pe", pe_v(tb))
                    for eh in range(2):
                        w8("s", sc, f"st{eh}", st_base[eh] + 128 + 16 * tb)
                        sc.activation(stg[eh][:, :], v_half(tb, eh), AF.Identity
                                      ).then_inc(SEM["ac"], 1)

            @blk.sync
            def _(s):
                for tb in range(TB):
                    for eh in range(2):
                        w8("sy", s, "ac", ac_v(tb, eh))
                        dest = kvi_v[tb * 256:(tb + 1) * 256, :].rearrange(
                            "(p a) q -> p a q", a=2)[:, eh:eh + 1, :]
                        s.dma_start(out=dest,
                                    in_=stg[eh][:, :].rearrange("p (a q) -> p a q", a=1)
                                    ).then_inc(SEM[f"st{eh}"], 16)
                        C[f"st{eh}"] += 16

        @blk.tensor
        def _(t):
            w8("t", t, "ac", ac_v(TB - 1, 1))
            for eb in range(EB):
                par = eb % 2
                w8("t", t, f"w{par}", w_base[par] + 16 * (8 + eb // 2 + 1))
                if eb >= 2:
                    w8("t", t, "ac", ac_q(eb - 2))
                ins = None
                for dc in range(DC):
                    for j in range(2):
                        ins = t.matmul(
                            p_pr[par][:, j * 512:(j + 1) * 512],
                            wtile[par][:, dc * 128:(dc + 1) * 128],
                            xnT[:, dc * T + j * 512: dc * T + (j + 1) * 512],
                            start=(dc == 0), stop=(dc == DC - 1))
                ins.then_inc(SEM["pe"], 1)

        @blk.scalar
        def _(sc):
            for eb in range(EB):
                par = eb % 2
                w8("s", sc, "pe", pe_q(eb))
                if eb < 2:
                    w8("s", sc, f"st{par}", st_base[par] + 128 + 128)
                else:
                    w8("s", sc, f"qw{par}", 16 * (eb // 2))
                sc.activation(stg[par][:, :], p_pr[par][:, :], AF.Identity,
                              bias=bqt[:, eb:eb + 1]).then_inc(SEM["ac"], 1)

        @blk.sync
        def _(s):
            for eb in range(EB):
                par = eb % 2
                w8("sy", s, "ac", ac_q(eb))
                s.dma_start(out=qT_d.ap()[eb * 128:(eb + 1) * 128, :],
                            in_=stg[par][:, :]).then_inc(SEM[f"qw{par}"], 16)
                C[f"qw{par}"] += 16
    C["pe"] += 40
    C["ac"] += 48
    pe_p3_vend = C["pe"] - 16
    pe_p3_end = C["pe"]

    sg = [sg0, sg1]
    with nc.psum_tensor("p_g0", [128, T], F32) as p_g0, \
         nc.psum_tensor("p_g1", [128, T], F32) as p_g1, \
         nc.psum_tensor("p_u0", [128, T], F32) as p_u0, \
         nc.psum_tensor("p_u1", [128, T], F32) as p_u1:
        p_g = [p_g0, p_g1]
        p_u = [p_u0, p_u1]
        pe0, ac0, dv0 = C["pe"], C["ac"], C["dv"]
        st_final = {0: C["st0"], 1: C["st1"]}

        @blk.gpsimd
        def _(g):
            w8("g", g, "st0", st_final[0])
            w8("g", g, "st1", st_final[1])
            g.collective_compute(
                "AllGather", ALU.bypass,
                replica_groups=[[0, 1, 2, 3], [4, 5, 6, 7]],
                ins=[kvi_d.ap().opt()], outs=[kvo_d.ap().opt()],
            ).then_inc(SEM["cc"], 1)
            C["cc"] = 1
            w8("g", g, "pe", pe_p3_vend)
            for k in range(4):
                g.dma_start(out=EXG2[k][:, :], in_=wg_d.ap()[k + 9, :, :]
                            ).then_inc(SEM[f"f{k}"], 16)
                g.dma_start(out=EXU2[k][:, :], in_=wu_d.ap()[k + 9, :, :]
                            ).then_inc(SEM[f"f{k}"], 16)
                C[f"f{k}"] += 32
            for ib in range(13, IB):
                s = gslot(ib)
                if ib >= 16:
                    w8("g", g, "pe", pe0 + 2 * (ib - 3) + 2)
                elif ib == 15:
                    w8("g", g, "pe", pe0 + 2)
                else:
                    w8("g", g, "pe", pe0)
                g.dma_start(out=GSLOT[s][:, :], in_=wg_d.ap()[ib, :, :]
                            ).then_inc(SEM[f"w{s}"], 16)
                g.dma_start(out=USLOT[s][:, :], in_=wu_d.ap()[ib, :, :]
                            ).then_inc(SEM[f"w{s}"], 16)
                C[f"w{s}"] += 32
                w_tgt[ib] = C[f"w{s}"]

        @blk.tensor
        def _(t):
            w8("t", t, "dv", dv0)
            w8("t", t, "ac", ac0)
            for ib in range(IB):
                par = ib % 2
                if 1 <= ib <= 8:
                    gtile, utile = EXG[ib - 1], EXU[ib - 1]
                    w8("t", t, f"e{ib - 1}", 32)
                elif 9 <= ib <= 12:
                    gtile, utile = EXG2[ib - 9], EXU2[ib - 9]
                    w8("t", t, f"f{ib - 9}", 32)
                else:
                    s = gslot(ib)
                    gtile, utile = GSLOT[s], USLOT[s]
                    w8("t", t, f"w{s}", w_tgt[ib])
                if ib >= 2:
                    w8("t", t, "ac", ac0 + ib - 1)
                    w8("t", t, "dv", dv0 + 3 * (ib - 2) + 3)
                ins = None
                for dc in range(DC):
                    for j in range(2):
                        ins = t.matmul(p_g[par][:, j * 512:(j + 1) * 512],
                                       gtile[:, dc * 128:(dc + 1) * 128],
                                       xnT[:, dc * T + j * 512: dc * T + (j + 1) * 512],
                                       start=(dc == 0), stop=(dc == DC - 1))
                ins.then_inc(SEM["pe"], 1)
                for dc in range(DC):
                    for j in range(2):
                        ins = t.matmul(p_u[par][:, j * 512:(j + 1) * 512],
                                       utile[:, dc * 128:(dc + 1) * 128],
                                       xnT[:, dc * T + j * 512: dc * T + (j + 1) * 512],
                                       start=(dc == 0), stop=(dc == DC - 1))
                ins.then_inc(SEM["pe"], 1)

        @blk.scalar
        def _(sc):
            for ib in range(IB):
                par = ib % 2
                w8("s", sc, "dv", dv0 + 3 * ib + 1)
                if ib >= 2:
                    w8("s", sc, "dv", dv0 + 3 * (ib - 2) + 3)
                sc.activation(sg[par][:, :], p_g[par][:, :], AF.Silu,
                              scale=gst[:, ib:ib + 1]).then_inc(SEM["ac"], 1)

        @blk.vector
        def _(v):
            w8("v", v, "st0", C["st0"])
            w8("v", v, "st1", C["st1"])
            w8("v", v, "qw0", C["qw0"])
            w8("v", v, "qw1", C["qw1"])
            for ib in range(IB):
                par = ib % 2
                w8("v", v, "pe", pe0 + 2 * ib + 1)
                v.tensor_add(p_g[par][:, :], p_g[par][:, :], rsb128[:, :]
                             ).then_inc(SEM["dv"], 1)
                w8("v", v, "pe", pe0 + 2 * ib + 2)
                v.tensor_add(p_u[par][:, :], p_u[par][:, :], rsb128[:, :]
                             ).then_inc(SEM["dv"], 1)
                w8("v", v, "ac", ac0 + ib + 1)
                w8("v", v, "dv", dv0 + 3 * ib + 2)
                v.scalar_tensor_tensor(
                    out=h_sb[:, ib * T:(ib + 1) * T], in0=p_u[par][:, :],
                    scalar=ust[:, ib:ib + 1], in1=sg[par][:, :],
                    op0=ALU.mult, op1=ALU.mult).then_inc(SEM["dv"], 1)
        C["pe"] += 128
        C["ac"] += 64
        C["dv"] += 3 * IB
    pe_p5_end = C["pe"]

    with nc.psum_tensor("p_hs", [1, T], F32) as p_hs:
        pe0, dv0 = C["pe"], C["dv"]

        @blk.tensor
        def _(t):
            w8("t", t, "dv", dv0)
            ins = None
            for ib in range(IB):
                for j in range(2):
                    ins = t.matmul(p_hs[:, j * 512:(j + 1) * 512], ones128[:, 0:1],
                                   h_sb[:, ib * T + j * 512: ib * T + (j + 1) * 512],
                                   start=(ib == 0), stop=(ib == IB - 1))
            ins.then_inc(SEM["pe"], 1)

        with nc.psum_tensor("p_bc2", [128, T], F32) as p_bc2:
            @blk.vector
            def _(v):
                w8("v", v, "pe", pe0 + 1)
                v.tensor_copy(vrow_f[:, :], p_hs[:, :]).then_inc(SEM["dv"], 1)
                w8("v", v, "dv", dv0 + 1)
                v.tensor_copy(vrow_b[:, :], vrow_f[:, :]).then_inc(SEM["dv"], 1)
                w8("v", v, "pe", pe0 + 2)
                v.tensor_copy(rsb128[:, :], p_bc2[:, :]).then_inc(SEM["dv"], 1)

            @blk.tensor
            def _(t):
                w8("t", t, "dv", dv0 + 2)
                ins = None
                for j in range(2):
                    ins = t.matmul(p_bc2[:, j * 512:(j + 1) * 512], negrow[:, :],
                                   vrow_b[:, j * 512:(j + 1) * 512],
                                   start=True, stop=True)
                ins.then_inc(SEM["pe"], 1)
        C["pe"] += 2
        C["dv"] += 3

    fst = [sg0, sg1]
    with nc.psum_tensor("p_d0", [128, T], F32) as p_d0, \
         nc.psum_tensor("p_d1", [128, T], F32) as p_d1:
        p_d = [p_d0, p_d1]
        pe0, ac0, dv0 = C["pe"], C["ac"], C["dv"]
        w_base = {k: C[f"w{k}"] for k in range(3)}
        fw_base = {0: C["fw0"], 1: C["fw1"]}
        pe_eb = lambda eb: pe0 + eb + 1

        @blk.gpsimd
        def _(g):
            for hj in range(2 * EB):
                eb, hf = divmod(hj, 2)
                sl = hj % 3
                if hj >= 3:
                    w8("g", g, "pe", pe_eb((hj - 3) // 2))
                else:
                    w8("g", g, "pe", pe_p5_end)
                g.dma_start(out=WD[sl][:, :],
                            in_=wd_d.ap()[eb, :, hf * 4096:(hf + 1) * 4096]
                            ).then_inc(SEM[f"w{sl}"], 16)
                C[f"w{sl}"] += 16

        @blk.tensor
        def _(t):
            w8("t", t, "dv", dv0)
            for eb in range(EB):
                par = eb % 2
                if eb >= 2:
                    w8("t", t, "ac", ac0 + eb - 1)
                ins = None
                for hf in range(2):
                    hj = 2 * eb + hf
                    sl = hj % 3
                    w8("t", t, f"w{sl}", w_base[sl] + 16 * (hj // 3 + 1))
                    for dc32 in range(32):
                        dc = hf * 32 + dc32
                        for j in range(2):
                            ins = t.matmul(p_d[par][:, j * 512:(j + 1) * 512],
                                           WD[sl][:, dc32 * 128:(dc32 + 1) * 128],
                                           h_sb[:, dc * T + j * 512: dc * T + (j + 1) * 512],
                                           start=(dc == 0), stop=(dc == IB - 1))
                ins.then_inc(SEM["pe"], 1)

        @blk.vector
        def _(v):
            for eb in range(EB):
                par = eb % 2
                w8("v", v, "pe", pe_eb(eb))
                if eb >= 2:
                    w8("v", v, "ac", ac0 + eb - 1)
                v.tensor_add(p_d[par][:, :], p_d[par][:, :], rsb128[:, :]
                             ).then_inc(SEM["dv"], 1)

        @blk.scalar
        def _(sc):
            for eb in range(EB):
                par = eb % 2
                w8("s", sc, "dv", dv0 + eb + 1)
                if eb >= 2:
                    w8("s", sc, f"fw{par}", fw_base[par] + 16 * (eb // 2))
                sc.activation(fst[par][:, :], p_d[par][:, :], AF.Identity,
                              scale=dst[:, eb:eb + 1], bias=bot[:, eb:eb + 1]
                              ).then_inc(SEM["ac"], 1)
                w8("s", sc, "ac", ac0 + eb + 1)
                sc.dma_start(out=ffn_d.ap()[eb * 128:(eb + 1) * 128, :],
                             in_=fst[par][:, :]).then_inc(SEM[f"fw{par}"], 16)
                C[f"fw{par}"] += 16
        C["pe"] += 16
        C["ac"] += 16
        C["dv"] += EB
    pe_p7_end = C["pe"]
    fw_end = {0: C["fw0"], 1: C["fw1"]}
    qw_end = {0: C["qw0"], 1: C["qw1"]}

    kh = [kh0, kh1]
    vh = [vh0, vh1]
    qh = [qh0, qh1]
    pT = [pT0, pT1]
    GROUPS = [(0, 3), (3, 6), (6, 9), (9, 12), (12, 15), (15, 18), (18, 21),
              (21, 24), (24, 27), (27, 30), (30, 32)]
    NG = len(GROUPS)
    NIDX = 2 * NH
    SLICES = [(0, 6), (6, 12), (12, 18), (18, 24), (24, 30), (30, 36), (36, 42),
              (42, 48), (48, 54), (54, 60), (60, 64)]

    pe0 = C["pe"]
    pe_cnt = pe0
    score_tgt = {}
    den_tgt = {}
    av_tgt = {}
    plan = []
    for idx in range(NIDX + 1):
        for g in range(NG):
            if idx < NIDX:
                pe_cnt += 1
                score_tgt[(idx, g)] = pe_cnt
                plan.append(("score", idx, g))
            if idx >= 1:
                pe_cnt += 1
                s0, s1 = SLICES[g]
                plan.append(("denav", idx - 1, g))
                if s0 <= 31 < s1 or s1 == 32:
                    pass
                if s1 >= 32 and s0 < 32:
                    den_tgt[idx - 1] = pe_cnt
                if s1 == 64:
                    av_tgt[idx - 1] = pe_cnt
    assert len(den_tgt) == NIDX and len(av_tgt) == NIDX
    pe_p8_end = pe_cnt

    ac0, dv0 = C["ac"], C["dv"]
    ac_exp = lambda idx, g: ac0 + idx * NG + g + 1
    ac_exp_end = lambda idx: ac0 + (idx + 1) * NG
    dv_rec = lambda idx: dv0 + 2 * idx + 1
    dv_omul = lambda idx: dv0 + 2 * idx + 2

    with nc.psum_tensor("p_sA", [128, 1536], F32) as p_sA, \
         nc.psum_tensor("p_sB", [128, 1536], F32) as p_sB, \
         nc.psum_tensor("p_den", [128, 512], F32) as p_den, \
         nc.psum_tensor("p_o", [128, 512], F32) as p_o:
        p_s = [p_sA, p_sB]
        kh_base = {0: C["kh0"], 1: C["kh1"]}
        vh_base = {0: C["vh0"], 1: C["vh1"]}
        qh_base = {0: C["qh0"], 1: C["qh1"]}

        @blk.sync
        def _(s):
            w8("sy", s, "cc", 1)
            w8("sy", s, "qw0", qw_end[0])
            w8("sy", s, "qw1", qw_end[1])
            for h in range(NH):
                par = h % 2
                if h >= 2:
                    w8("sy", s, "pe", av_tgt[2 * (h - 2) + 1])
                else:
                    w8("sy", s, "pe", pe_p5_end)
                for r in range(4):
                    s.dma_start(out=kh[par][:, r * T:(r + 1) * T],
                                in_=kvo_d.ap()[r * 4096 + h * 128: r * 4096 + (h + 1) * 128, :]
                                ).then_inc(SEM[f"kh{par}"], 16)
                    C[f"kh{par}"] += 16
                for r in range(4):
                    src_v = kvo_d.ap()[r * 4096 + D: r * 4096 + 2 * D,
                                       (h % 8) * 128:(h % 8 + 1) * 128].rearrange(
                        "(tt p a) e -> p tt a e", tt=8, p=128)[:, :, h // 8: h // 8 + 1, :]
                    s.dma_start(out=vh[par][:, r * T:(r + 1) * T].rearrange(
                                    "p (tt a e) -> p tt a e", tt=8, a=1),
                                in_=src_v).then_inc(SEM[f"vh{par}"], 16)
                    C[f"vh{par}"] += 16
                s.dma_start(out=qh[par][:, :], in_=qT_d.ap()[h * 128:(h + 1) * 128, :]
                            ).then_inc(SEM[f"qh{par}"], 16)
                C[f"qh{par}"] += 16

        @blk.tensor
        def _(t):
            w8("t", t, "ac", ac0)
            for kind, idx, g in plan:
                h, j = divmod(idx, 2)
                par = h % 2
                if kind == "score":
                    gg = idx * NG + g
                    if g == 0 and j == 0:
                        w8("t", t, f"kh{par}", kh_base[par] + 64 * (h // 2 + 1))
                        w8("t", t, f"qh{par}", qh_base[par] + 16 * (h // 2 + 1))
                    if gg >= 2:
                        w8("t", t, "ac", ac_exp(*divmod(gg - 2, NG)))
                    b0, b1 = GROUPS[g]
                    ins = None
                    for b in range(b0, b1):
                        ins = t.matmul(p_s[gg % 2][:, (b - b0) * 512:(b - b0 + 1) * 512],
                                       kh[par][:, b * 128:(b + 1) * 128],
                                       qh[par][:, j * 512:(j + 1) * 512],
                                       start=True, stop=True)
                    ins.then_inc(SEM["pe"], 1)
                else:
                    s0, s1 = SLICES[g]
                    if g == 0:
                        w8("t", t, "ac", ac_exp_end(idx))
                        w8("t", t, f"vh{par}", vh_base[par] + 64 * (h // 2 + 1))
                    ins = None
                    for e in range(s0, s1):
                        if e == 0 and idx >= 1:
                            w8("t", t, "dv", dv_rec(idx - 1))
                        if e == 32 and idx >= 1:
                            w8("t", t, "dv", dv_omul(idx - 1))
                        if e < 32:
                            ins = t.matmul(p_den[:, :], ones128[:, :],
                                           pT[idx % 2][:, e * 512:(e + 1) * 512],
                                           start=(e == 0), stop=(e == 31))
                        else:
                            b = e - 32
                            ins = t.matmul(p_o[:, :], vh[par][:, b * 128:(b + 1) * 128],
                                           pT[idx % 2][:, b * 512:(b + 1) * 512],
                                           start=(b == 0), stop=(b == 31))
                    ins.then_inc(SEM["pe"], 1)

        @blk.scalar
        def _(sc):
            for idx in range(NIDX):
                if idx == NIDX - 1:
                    w8("s", sc, "fw0", fw_end[0])
                    w8("s", sc, "fw1", fw_end[1])
                    for eb in range(2):
                        sc.dma_start(out=(fti0 if eb == 0 else fti1)[:, :],
                                     in_=ffn_d.ap()[eb * 128:(eb + 1) * 128, :]
                                     ).then_inc(SEM[f"fr{eb % 2}"], 16)
                        C[f"fr{eb % 2}"] += 16
                for g, (b0, b1) in enumerate(GROUPS):
                    w8("s", sc, "pe", score_tgt[(idx, g)])
                    if g == 0 and idx >= 2:
                        w8("s", sc, "pe", av_tgt[idx - 2])
                    sc.activation(pT[idx % 2][:, b0 * 512:b1 * 512],
                                  p_s[(idx * NG + g) % 2][:, 0:(b1 - b0) * 512], AF.Exp,
                                  scale=SC).then_inc(SEM["ac"], 1)
            w8("s", sc, "pe", av_tgt[NIDX - 2])
            for eb in range(8):
                sc.dma_start(
                    out=xf1[:, eb * T:(eb + 1) * T].rearrange(
                        "p (tb e) -> p tb e", e=128),
                    in_=x_d.ap()[:, eb * 128:(eb + 1) * 128].rearrange(
                        "(tb p) e -> p tb e", p=128)).then_inc(SEM["xf"], 16)
                C["xf"] += 16

        @blk.vector
        def _(v):
            for idx in range(NIDX):
                h, j = divmod(idx, 2)
                w8("v", v, "pe", den_tgt[idx])
                if idx >= 1:
                    w8("v", v, "dv", dv_omul(idx - 1))
                v.reciprocal(recipb[:, :], p_den[:, :]).then_inc(SEM["dv"], 1)
                w8("v", v, "pe", av_tgt[idx])
                w8("v", v, "dv", dv_rec(idx))
                v.tensor_mul(oT[:, h * T + j * 512: h * T + (j + 1) * 512],
                             p_o[:, :], recipb[:, :]).then_inc(SEM["dv"], 1)
        C["pe"] = pe_p8_end
        C["ac"] += NIDX * NG
        C["dv"] += NIDX * 2

    fti = [fti0, fti1]
    ote = [ote0, ote1]
    with nc.psum_tensor("p_w0", [128, T], F32) as p_w0, \
         nc.psum_tensor("p_w1", [128, T], F32) as p_w1, \
         nc.psum_tensor("p_f0", [128, 128], F32) as p_f0, \
         nc.psum_tensor("p_f1", [128, 128], F32) as p_f1, \
         nc.psum_tensor("p_f2", [128, 128], F32) as p_f2, \
         nc.psum_tensor("p_f3", [128, 128], F32) as p_f3:
        p_w = [p_w0, p_w1]
        p_f = [p_f0, p_f1, p_f2, p_f3]
        pe0, ac0, dv0 = C["pe"], C["ac"], C["dv"]
        w_base = {0: C["w0"], 1: C["w1"]}
        ou_base = {i: C[f"ou{i}"] for i in range(8)}
        pe_proj = {}
        pe_tp = {}
        cnt = pe0
        for eb in range(EB):
            cnt += 1
            pe_proj[eb] = cnt
            if eb >= 1:
                for tb in range(TB):
                    cnt += 1
                    pe_tp[(eb - 1, tb)] = cnt
        for tb in range(TB):
            cnt += 1
            pe_tp[(EB - 1, tb)] = cnt
        pe_p9_end = cnt
        dv_ote = lambda eb: dv0 + eb * 9 + 1
        dv_of = lambda eb, tb: dv0 + eb * 9 + 1 + tb + 1

        @blk.gpsimd
        def _(g):
            for eb in range(EB):
                par = eb % 2
                if eb >= 2:
                    w8("g", g, "pe", pe_proj[eb - 2])
                else:
                    w8("g", g, "pe", pe_p7_end)
                g.dma_start(out=wtile[par][:, :], in_=wo_d.ap()[eb, :, :]
                            ).then_inc(SEM[f"w{par}"], 16)
                C[f"w{par}"] += 16

        @blk.scalar
        def _(sc):
            w8("s", sc, "pe", pe_p8_end)
            for eb in range(2, EB):
                par = eb % 2
                w8("s", sc, "dv", dv_ote(eb - 2))
                sc.dma_start(out=fti[par][:, :],
                             in_=ffn_d.ap()[eb * 128:(eb + 1) * 128, :]
                             ).then_inc(SEM[f"fr{par}"], 16)
                C[f"fr{par}"] += 16
                if eb == 2:
                    for eb2 in range(8, EB):
                        sc.dma_start(
                            out=xf2[:, (eb2 - 8) * T:(eb2 - 7) * T].rearrange(
                                "p (tb e) -> p tb e", e=128),
                            in_=x_d.ap()[:, eb2 * 128:(eb2 + 1) * 128].rearrange(
                                "(tb p) e -> p tb e", p=128)).then_inc(SEM["xf2"], 16)
                        C["xf2"] += 16

        @blk.tensor
        def _(t):
            w8("t", t, "dv", dv0)
            def proj(eb):
                par = eb % 2
                w8("t", t, f"w{par}", w_base[par] + 16 * (eb // 2 + 1))
                if eb >= 2:
                    w8("t", t, "dv", dv_ote(eb - 2))
                ins = None
                for dc in range(DC):
                    for j in range(2):
                        ins = t.matmul(p_w[par][:, j * 512:(j + 1) * 512],
                                       wtile[par][:, dc * 128:(dc + 1) * 128],
                                       oT[:, dc * T + j * 512: dc * T + (j + 1) * 512],
                                       start=(dc == 0), stop=(dc == DC - 1))
                ins.then_inc(SEM["pe"], 1)

            def transp(eb):
                par = eb % 2
                w8("t", t, "dv", dv_ote(eb))
                for tb in range(TB):
                    k = eb * TB + tb
                    if k >= 4:
                        pk_eb, pk_tb = divmod(k - 4, TB)
                        w8("t", t, "dv", dv_of(pk_eb, pk_tb))
                    t.transpose(p_f[tb % 4][:, :], ote[par][:, tb * 128:(tb + 1) * 128],
                                ident_f[:, :]).then_inc(SEM["pe"], 1)

            for eb in range(EB):
                proj(eb)
                if eb >= 1:
                    transp(eb - 1)
            transp(EB - 1)

        @blk.vector
        def _(v):
            for eb in range(EB):
                par = eb % 2
                w8("v", v, "pe", pe_proj[eb])
                w8("v", v, f"fr{par}", 16 * (eb // 2 + 1))
                if eb >= 2:
                    w8("v", v, "pe", pe_tp[(eb - 2, TB - 1)])
                v.tensor_add(ote[par][:, :], p_w[par][:, :], fti[par][:, :]
                             ).then_inc(SEM["dv"], 1)
                w8("v", v, "xf" if eb < 8 else "xf2", 128)
                xsl = (xf1[:, eb * T + 0:eb * T + T] if eb < 8
                       else xf2[:, (eb - 8) * T:(eb - 7) * T])
                for tb in range(TB):
                    k = eb * TB + tb
                    w8("v", v, "pe", pe_tp[(eb, tb)])
                    if k >= 8:
                        pk = k - 8
                        w8("v", v, f"ou{pk % 8}", ou_base[pk % 8] + 16 * (pk // 8 + 1))
                    v.tensor_add(OF[k % 8][:, :], p_f[tb % 4][:, :],
                                 xsl[:, tb * 128:(tb + 1) * 128]).then_inc(SEM["dv"], 1)

        @blk.sync
        def _(s):
            for eb in range(EB):
                for tb in range(TB):
                    k = eb * TB + tb
                    w8("sy", s, "dv", dv_of(eb, tb))
                    s.dma_start(out=out_d.ap()[tb * 128:(tb + 1) * 128,
                                               eb * 128:(eb + 1) * 128],
                                in_=OF[k % 8][:, :]).then_inc(SEM[f"ou{k % 8}"], 16)
                    C[f"ou{k % 8}"] += 16
            for i in range(8):
                w8("sy", s, f"ou{i}", ou_base[i] + 16 * 16)
        C["pe"] = pe_p9_end
        C["dv"] += EB * 9

    es.close()
    return nc


_nc_cache = None


def _get_nc():
    global _nc_cache
    if _nc_cache is None:
        _nc_cache = build_nc()
    return _nc_cache


def _block_T(w, nb_out, nb_in):
    wT = np.ascontiguousarray(w.T)
    a = wT.reshape(nb_in, 128, nb_out, 128)
    return np.ascontiguousarray(a.transpose(2, 1, 0, 3).reshape(nb_out, 128, nb_in * 128))


def kernel(**inp):
    x = np.ascontiguousarray(np.asarray(inp["L0_attn_x"], dtype=np.float32))
    wq = np.asarray(inp["L0_attn_wq"], np.float32)
    wk = np.asarray(inp["L0_attn_wk"], np.float32)
    wv = np.asarray(inp["L0_attn_wv"], np.float32)
    wo = np.asarray(inp["L0_attn_wo"], np.float32)
    bq = np.ascontiguousarray(np.asarray(inp["L0_attn_bq"], np.float32))
    bk = np.ascontiguousarray(np.asarray(inp["L0_attn_bk"], np.float32))
    bv = np.ascontiguousarray(np.asarray(inp["L0_attn_bv"], np.float32))
    bo = np.ascontiguousarray(np.asarray(inp["L0_attn_bo"], np.float32))
    gw = np.asarray(inp["L0_ffn_gate_w"], np.int32)
    uw = np.asarray(inp["L0_ffn_up_w"], np.int32)
    dw = np.asarray(inp["L0_ffn_down_w"], np.int32)
    gs = np.ascontiguousarray(np.asarray(inp["L0_ffn_gate_scale"], np.float32))
    us = np.ascontiguousarray(np.asarray(inp["L0_ffn_up_scale"], np.float32))
    dsc = np.ascontiguousarray(np.asarray(inp["L0_ffn_down_scale"], np.float32))

    shared = dict(
        wq=_block_T(wq, EB, DC), wk=_block_T(wk, EB, DC),
        wv=np.ascontiguousarray(wv.T.reshape(DC, 128, D)),
        wo=_block_T(wo, EB, DC),
        bq=bq, bk=bk, bv=bv, bo=bo,
        wg=_block_T(gw, IB, DC), wu=_block_T(uw, IB, DC),
        wd=_block_T(dw, EB, IB),
        gs=gs, us=us, dsc=dsc,
        ident=np.eye(128, dtype=np.float32),
    )
    xs = x.reshape(8, T, D)
    in_maps = [dict(shared, x=np.ascontiguousarray(xs[c])) for c in range(8)]

    nc = _get_nc()
    res = run_bass_kernel_spmd(nc, in_maps, core_ids=list(range(8)))
    outs = [np.asarray(res.results[c]["out"]) for c in range(8)]
    return np.concatenate(outs, axis=0).reshape(2, 4096, 2048)


# version 18
# speedup vs baseline: 1.0315x; 1.0315x over previous
import sys
sys.path.insert(0, "/opt/trn_rl_repo")
import contextlib
import numpy as np
import concourse.bass as bass
import concourse.mybir as mybir
from concourse.bass_utils import run_bass_kernel_spmd

F32 = mybir.dt.float32
BF16 = mybir.dt.bfloat16
I32 = mybir.dt.int32
AF = mybir.ActivationFunctionType
ALU = mybir.AluOpType
AX = mybir.AxisListType

T = 1024
D = 2048
II = 8192
NH = 16
TB = 8
DC = 16
IB = 64
EB = 16
SC = 0.08838834764831845
EPS = 1e-5


def build_nc():
    nc = bass.Bass("TRN2", target_bir_lowering=False, debug=False)

    x_d = nc.dram_tensor("x", [T, D], F32, kind="ExternalInput")
    wq_d = nc.dram_tensor("wq", [EB, 128, D], F32, kind="ExternalInput")
    wk_d = nc.dram_tensor("wk", [EB, 128, D], F32, kind="ExternalInput")
    wv_d = nc.dram_tensor("wv", [EB, 128, D], F32, kind="ExternalInput")
    wo_d = nc.dram_tensor("wo", [EB, 128, D], F32, kind="ExternalInput")
    bq_d = nc.dram_tensor("bq", [D], F32, kind="ExternalInput")
    bk_d = nc.dram_tensor("bk", [D], F32, kind="ExternalInput")
    bv_d = nc.dram_tensor("bv", [D], F32, kind="ExternalInput")
    bo_d = nc.dram_tensor("bo", [D], F32, kind="ExternalInput")
    wg_d = nc.dram_tensor("wg", [IB, 128, D], I32, kind="ExternalInput")
    wu_d = nc.dram_tensor("wu", [IB, 128, D], I32, kind="ExternalInput")
    wd_d = nc.dram_tensor("wd", [EB, 128, II], I32, kind="ExternalInput")
    gs_d = nc.dram_tensor("gs", [II], F32, kind="ExternalInput")
    us_d = nc.dram_tensor("us", [II], F32, kind="ExternalInput")
    ds_d = nc.dram_tensor("dsc", [D], F32, kind="ExternalInput")
    id_d = nc.dram_tensor("ident", [128, 128], F32, kind="ExternalInput")

    out_d = nc.dram_tensor("out", [T, D], F32, kind="ExternalOutput")

    qT_d = nc.dram_tensor("qTd", [D, T], BF16, kind="Internal")
    kvi_d = nc.dram_tensor("kv_in", [2 * D, T], BF16, kind="Internal")
    kvo_d = nc.dram_tensor("kv_out", [8 * D, T], BF16, kind="Internal")
    ffn_d = nc.dram_tensor("ffnd", [D, T], F32, kind="Internal")

    BASE = 17920
    off = [BASE]

    def sb(name, shape, dtype, at=None):
        nbytes = int(np.prod(shape[1:])) * (4 if dtype in (F32, I32) else 2)
        if at is None:
            at = off[0]
            off[0] = at + ((nbytes + 63) // 64) * 64
        return nc.alloc_sbuf_tensor_at(name, shape, dtype, offset=at)

    ident_f = sb("ident_f", [128, 128], F32)
    ident_b = sb("ident_b", [128, 128], BF16)
    ones128 = sb("ones128", [128, 128], BF16)
    negrow = sb("negrow", [1, 128], BF16)
    bqt = sb("bqt", [128, EB], F32)
    bkt = sb("bkt", [128, EB], F32)
    bvt = sb("bvt", [128, EB], F32)
    bot = sb("bot", [128, EB], F32)
    gst = sb("gst", [128, IB], F32)
    ust = sb("ust", [128, IB], F32)
    dst = sb("dst", [128, EB], F32)
    vrow_f = sb("vrow_f", [1, T], F32)
    vrow_b = sb("vrow_b", [1, T], BF16)
    stat0 = sb("stat0", [128, 8], F32)
    stat1 = sb("stat1", [128, 8], F32)
    sg0 = sb("sg0", [128, T], F32)
    sg1 = sb("sg1", [128, T], F32)
    _recip_at = off[0]
    recipb = sb("recipb", [128, 512], F32)
    rsb128 = sb("rsb128", [128, T], BF16, at=_recip_at)
    qh0 = sb("qh0", [128, T], BF16)
    qh1 = sb("qh1", [128, T], BF16)

    wbuf = sb("wbuf", [128, 12288], BF16)
    WA = [wbuf[:, 0:2048], wbuf[:, 2048:4096], wbuf[:, 4096:6144], wbuf[:, 6144:8192]]
    WD = [wbuf[:, 0:4096], wbuf[:, 4096:8192], wbuf[:, 8192:12288]]

    R0 = off[0]
    assert R0 + 163840 <= 229344, f"sbuf overflow R0={R0}"

    xnT = sb("xnT", [128, DC * T], BF16, at=R0)
    h_sb = sb("h_sb", [128, IB * T], BF16, at=R0 + 32768)
    bq16 = sb("bq16", [16, 128], F32, at=R0 + 36864)
    bk16 = sb("bk16", [16, 128], F32, at=R0 + 37376)
    bv16 = sb("bv16", [16, 128], F32, at=R0 + 37888)
    bo16 = sb("bo16", [16, 128], F32, at=R0 + 38400)
    gs64 = sb("gs64", [64, 128], F32, at=R0 + 38912)
    us64 = sb("us64", [64, 128], F32, at=R0 + 39424)
    ds16 = sb("ds16", [16, 128], F32, at=R0 + 39936)
    wvres = sb("wvres", [128, DC * 2048], BF16, at=R0 + 65536)
    xsq0 = sb("xsq0", [128, D], F32, at=R0 + 40960)
    bv_row = sb("bv_row", [1, D], BF16, at=R0 + 40960)
    xsq1 = sb("xsq1", [128, D], F32, at=R0 + 49152)
    xn_t0 = sb("xn_t0", [128, D], BF16, at=R0 + 57344)
    xn_t1 = sb("xn_t1", [128, D], BF16, at=R0 + 61440)
    stg0 = sb("stg0", [128, T], BF16, at=R0 + 32768)
    stg1 = sb("stg1", [128, T], BF16, at=R0 + 34816)
    kh0 = sb("kh0", [128, 4096], BF16, at=R0 + 0)
    kh1 = sb("kh1", [128, 4096], BF16, at=R0 + 8192)
    vh0 = sb("vh0", [128, 4096], BF16, at=R0 + 16384)
    vh1 = sb("vh1", [128, 4096], BF16, at=R0 + 24576)
    oT = sb("oT", [128, NH * T], BF16, at=R0 + 32768)
    pT0 = sb("pT0", [128, 32 * 512], BF16, at=R0 + 65536)
    pT1 = sb("pT1", [128, 32 * 512], BF16, at=R0 + 98304)
    fti0 = sb("fti0", [128, T], F32, at=R0 + 0)
    fti1 = sb("fti1", [128, T], F32, at=R0 + 4096)
    ote0 = sb("ote0", [128, T], F32, at=R0 + 8192)
    ote1 = sb("ote1", [128, T], F32, at=R0 + 12288)
    OF = [sb(f"of{i}", [128, 128], F32, at=R0 + 16384 + 512 * i) for i in range(8)]
    xf = sb("xf", [128, TB * D], F32, at=R0 + 98304)
    xf1 = sb("xf1", [128, 8 * T], F32, at=R0 + 65536)
    xf2 = sb("xf2", [128, 8 * T], F32, at=R0 + 98304)
    EXG = [sb(f"exg{k}", [128, 2048], BF16, at=R0 + 98304 + k * 8192) for k in range(8)]
    EXU = [sb(f"exu{k}", [128, 2048], BF16, at=R0 + 98304 + k * 8192 + 4096) for k in range(8)]
    EXG2 = [sb(f"exg2_{k}", [128, 2048], BF16, at=R0 + 65536 + k * 8192) for k in range(4)]
    EXU2 = [sb(f"exu2_{k}", [128, 2048], BF16, at=R0 + 65536 + k * 8192 + 4096) for k in range(4)]

    es = contextlib.ExitStack()
    sem_names = (["cl", "xl", "xl0", "id", "wv", "e0", "e1", "e2", "e3", "e4", "e5", "e6", "e7", "f0", "f1", "f2", "f3", "w0", "w1", "w2", "st0", "st1", "qw0", "qw1",
                  "cc", "kh0", "kh1", "vh0", "vh1", "qh0", "qh1", "fw0", "fw1",
                  "fr0", "fr1", "xf", "xf2", "pe", "ac", "dv"]
                 + [f"ou{i}" for i in range(8)])
    SEM = {n: es.enter_context(nc.semaphore(n)) for n in sem_names}
    blk = es.enter_context(nc.Block())

    C = {n: 0 for n in sem_names}
    LASTW = {}

    def w8(ename, eng, s, val):
        if val <= 0:
            return
        if LASTW.get((ename, s), 0) >= val:
            return
        LASTW[(ename, s)] = val
        eng.wait_ge(SEM[s], val)

    @blk.sync
    def _(s):
        s.dma_start(out=xf[:, 0:2 * D].rearrange("p (n d) -> p n d", n=2),
                    in_=x_d.ap()[0:256, :].rearrange("(n p) d -> p n d", p=128)
                    ).then_inc(SEM["xl0"], 16)
        s.dma_start(out=ident_f[:, :], in_=id_d[:, :]).then_inc(SEM["id"], 16)
        for dst_t, src in ((bq16, bq_d), (bk16, bk_d), (bv16, bv_d), (bo16, bo_d),
                           (gs64, gs_d), (us64, us_d), (ds16, ds_d)):
            s.dma_start(out=dst_t[:, :], in_=src.ap().rearrange("(c p) -> c p", p=128)
                        ).then_inc(SEM["cl"], 16)
        s.dma_start(out=xf[:, 2 * D:].rearrange("p (n d) -> p n d", n=6),
                    in_=x_d.ap()[256:1024, :].rearrange("(n p) d -> p n d", p=128)
                    ).then_inc(SEM["xl"], 16)
    C["xl0"] = 16
    C["xl"] = 16
    C["id"] = 16
    C["cl"] = 112

    @blk.vector
    def _(v):
        w8("v", v, "id", 16)
        v.tensor_copy(ident_b[:, :], ident_f[:, :]).then_inc(SEM["dv"], 1)
        v.memset(ones128[:, :], 1.0).then_inc(SEM["dv"], 1)
        v.memset(negrow[:, :], -128.0).then_inc(SEM["dv"], 1)
    C["dv"] = 3

    p0src = [(bq16, 16, bqt, EB), (bk16, 16, bkt, EB), (bv16, 16, bvt, EB),
             (bo16, 16, bot, EB), (gs64, 64, gst, IB), (us64, 64, ust, IB),
             (ds16, 16, dst, EB)]

    with nc.psum_tensor("p_b0", [128, 64], F32) as p_b0, \
         nc.psum_tensor("p_b1", [128, 64], F32) as p_b1:
        p_b = [p_b0, p_b1]
        pe0, ac0, dv0 = C["pe"], C["ac"], C["dv"]

        @blk.tensor
        def _(t):
            w8("t", t, "cl", 112)
            w8("t", t, "id", 16)
            w8("t", t, "dv", 3)
            for k, (src_t, kk, dst_t, ncol) in enumerate(p0src):
                if k >= 2:
                    w8("t", t, "dv", dv0 + k - 1)
                t.transpose(p_b[k % 2][:, 0:kk], src_t[:, :], ident_f[0:kk, 0:kk]
                            ).then_inc(SEM["pe"], 1)

        @blk.vector
        def _(v):
            for k, (src_t, kk, dst_t, ncol) in enumerate(p0src):
                w8("v", v, "pe", pe0 + k + 1)
                v.tensor_copy(dst_t[:, 0:ncol], p_b[k % 2][:, 0:ncol]).then_inc(SEM["dv"], 1)
        C["pe"] += 7
        C["dv"] += 7
    dv_bias_end = C["dv"]

    xsq = [xsq0, xsq1]
    xn_t = [xn_t0, xn_t1]
    stat = [stat0, stat1]
    with nc.psum_tensor("p_x0", [128, 128], BF16) as p_x0, \
         nc.psum_tensor("p_x1", [128, 128], BF16) as p_x1, \
         nc.psum_tensor("p_x2", [128, 128], BF16) as p_x2, \
         nc.psum_tensor("p_x3", [128, 128], BF16) as p_x3:
        p_x = [p_x0, p_x1, p_x2, p_x3]
        pe0, ac0, dv0 = C["pe"], C["ac"], C["dv"]

        dv_tbl = {}
        ac_tbl = {}
        cnt = dv0
        for n in range(TB):
            for op in ("r1", "msq", "m", "mm", "var"):
                cnt += 1
                dv_tbl[(op, n)] = cnt
            if n >= 1:
                for dc in range(0, DC, 2):
                    cnt += 1
                    dv_tbl[("ev", n - 1, dc)] = cnt
            for op in ("rstd", "xn"):
                cnt += 1
                dv_tbl[(op, n)] = cnt
        for dc in range(0, DC, 2):
            cnt += 1
            dv_tbl[("ev", TB - 1, dc)] = cnt
        dv_end_p1 = cnt
        cnt = ac0
        for n in range(TB):
            cnt += 1
            ac_tbl[("sq", n)] = cnt
            if n >= 1:
                for dc in range(1, DC, 2):
                    cnt += 1
                    ac_tbl[("ev", n - 1, dc)] = cnt
            cnt += 1
            ac_tbl[("sd", n)] = cnt
        for dc in range(1, DC, 2):
            cnt += 1
            ac_tbl[("ev", TB - 1, dc)] = cnt
        ac_end_p1 = cnt
        pe_tr = lambda n, dc: pe0 + 16 * n + dc + 1

        @blk.scalar
        def _(sc):
            def evacs_a(n):
                for dc in range(1, DC, 2):
                    w8("s", sc, "pe", pe_tr(n, dc))
                    sc.activation(xnT[:, dc * T + n * 128: dc * T + (n + 1) * 128],
                                  p_x[dc % 4][:, :], AF.Identity).then_inc(SEM["ac"], 1)

            for n in range(TB):
                w8("s", sc, "xl0" if n < 2 else "xl", 16)
                if n >= 2:
                    w8("s", sc, "ac", ac_tbl[("sq", n - 2)])
                    w8("s", sc, "dv", dv_tbl[("msq", n - 2)])
                sc.activation(xsq[n % 2][:, :], xf[:, n * D:(n + 1) * D], AF.Square,
                              accum_out=stat[n % 2][:, 1:2]).then_inc(SEM["ac"], 1)
                if n >= 1:
                    evacs_a(n - 1)
                w8("s", sc, "dv", dv_tbl[("var", n)])
                sc.activation(stat[n % 2][:, 6:7], stat[n % 2][:, 5:6], AF.Sqrt
                              ).then_inc(SEM["ac"], 1)
            evacs_a(TB - 1)

        @blk.vector
        def _(v):
            def stats(n):
                st = stat[n % 2]
                w8("v", v, "xl0" if n < 2 else "xl", 16)
                v.reduce_sum(st[:, 0:1], xf[:, n * D:(n + 1) * D], axis=AX.X).then_inc(SEM["dv"], 1)
                w8("v", v, "ac", ac_tbl[("sq", n)])
                v.tensor_scalar_mul(st[:, 3:4], st[:, 1:2], 1.0 / D).then_inc(SEM["dv"], 1)
                w8("v", v, "dv", dv_tbl[("r1", n)])
                v.tensor_scalar_mul(st[:, 2:3], st[:, 0:1], 1.0 / D).then_inc(SEM["dv"], 1)
                w8("v", v, "dv", dv_tbl[("m", n)])
                v.tensor_mul(st[:, 4:5], st[:, 2:3], st[:, 2:3]).then_inc(SEM["dv"], 1)
                w8("v", v, "dv", dv_tbl[("mm", n)])
                v.tensor_scalar(out=st[:, 5:6], in0=st[:, 3:4], scalar1=st[:, 4:5],
                                scalar2=EPS, op0=ALU.subtract, op1=ALU.add
                                ).then_inc(SEM["dv"], 1)
                if n >= 1:
                    evacs(n - 1)
                w8("v", v, "ac", ac_tbl[("sd", n)])
                v.reciprocal(st[:, 7:8], st[:, 6:7]).then_inc(SEM["dv"], 1)
                w8("v", v, "dv", dv_tbl[("rstd", n)])
                v.tensor_scalar(out=xn_t[n % 2][:, :], in0=xf[:, n * D:(n + 1) * D],
                                scalar1=st[:, 2:3], scalar2=st[:, 7:8],
                                op0=ALU.subtract, op1=ALU.mult).then_inc(SEM["dv"], 1)

            def evacs(n):
                for dc in range(0, DC, 2):
                    w8("v", v, "pe", pe_tr(n, dc))
                    v.tensor_copy(xnT[:, dc * T + n * 128: dc * T + (n + 1) * 128],
                                  p_x[dc % 4][:, :]).then_inc(SEM["dv"], 1)

            for n in range(TB):
                stats(n)
            evacs(TB - 1)

        @blk.tensor
        def _(t):
            w8("t", t, "dv", dv_bias_end)
            for n in range(TB):
                w8("t", t, "dv", dv_tbl[("xn", n)])
                for dc in range(DC):
                    k = n * 16 + dc
                    if k >= 4:
                        pn, pdc = divmod(k - 4, 16)
                        if pdc % 2 == 0:
                            w8("t", t, "dv", dv_tbl[("ev", pn, pdc)])
                        else:
                            w8("t", t, "ac", ac_tbl[("ev", pn, pdc)])
                    t.transpose(p_x[dc % 4][:, :], xn_t[n % 2][:, dc * 128:(dc + 1) * 128],
                                ident_b[:, :]).then_inc(SEM["pe"], 1)

        C["pe"] += 128
        C["ac"] = ac_end_p1
        C["dv"] = dv_end_p1


    with nc.psum_tensor("p_rs", [1, T], F32) as p_rs:
        pe0, dv0 = C["pe"], C["dv"]
        ac0_p2 = C["ac"]

        @blk.tensor
        def _(t):
            w8("t", t, "dv", dv0)
            w8("t", t, "ac", ac0_p2)
            ins = None
            for dc in range(DC):
                for j in range(2):
                    ins = t.matmul(p_rs[:, j * 512:(j + 1) * 512], ones128[:, 0:1],
                                   xnT[:, dc * T + j * 512: dc * T + (j + 1) * 512],
                                   start=(dc == 0), stop=(dc == DC - 1))
            ins.then_inc(SEM["pe"], 1)

        with nc.psum_tensor("p_bc", [128, T], F32) as p_bc:
            @blk.vector
            def _(v):
                w8("v", v, "pe", pe0 + 1)
                v.tensor_copy(vrow_f[:, :], p_rs[:, :]).then_inc(SEM["dv"], 1)
                w8("v", v, "dv", dv0 + 1)
                v.tensor_copy(vrow_b[:, :], vrow_f[:, :]).then_inc(SEM["dv"], 1)
                w8("v", v, "pe", pe0 + 2)
                v.tensor_copy(rsb128[:, :], p_bc[:, :]).then_inc(SEM["dv"], 1)

            @blk.tensor
            def _(t):
                w8("t", t, "dv", dv0 + 2)
                ins = None
                for j in range(2):
                    ins = t.matmul(p_bc[:, j * 512:(j + 1) * 512], negrow[:, :],
                                   vrow_b[:, j * 512:(j + 1) * 512],
                                   start=True, stop=True)
                ins.then_inc(SEM["pe"], 1)
        C["pe"] += 2
        C["dv"] += 3

    wtile = [WA[0], WA[1]]
    stg = [stg0, stg1]
    GSLOT = [wbuf[:, s * 4096: s * 4096 + 2048] for s in range(3)]
    USLOT = [wbuf[:, s * 4096 + 2048: (s + 1) * 4096] for s in range(3)]
    gslot = lambda ib: (ib + 2) % 3
    w_tgt = {}

    pe0, ac0, dv0 = C["pe"], C["ac"], C["dv"]
    w_base = {0: C["w0"], 1: C["w1"]}
    st_base = {0: C["st0"], 1: C["st1"]}
    pe_k = lambda eb: pe0 + eb + 1
    pe_v = lambda tb: pe0 + 16 + tb + 1
    pe_q = lambda eb: pe0 + 24 + eb + 1
    ac_k = lambda eb: ac0 + eb + 1
    ac_v = lambda tb, eh: ac0 + 16 + 2 * tb + eh + 1
    ac_q = lambda eb: ac0 + 32 + eb + 1
    kvi_v = kvi_d.ap()[D:2 * D, :]

    @blk.gpsimd
    def _(g):
        for eb in range(EB):
            par = eb % 2
            if eb >= 2:
                w8("g", g, "pe", pe_k(eb - 2))
            g.dma_start(out=wtile[par][:, :], in_=wk_d.ap()[eb, :, :]
                        ).then_inc(SEM[f"w{par}"], 16)
            C[f"w{par}"] += 16
            g.dma_start(out=wvres[:, eb * 2048:(eb + 1) * 2048],
                        in_=wv_d.ap()[eb, :, :]).then_inc(SEM["wv"], 16)
            C["wv"] += 16
        w8("g", g, "dv", dv0)
        g.dma_start(out=bv_row[:, :], in_=bv_d.ap().rearrange("(a q) -> a q", a=1)
                    ).then_inc(SEM["wv"], 16)
        C["wv"] += 16
        for eb in range(EB):
            par = eb % 2
            if eb < 2:
                w8("g", g, "pe", pe_k(14 + eb))
            else:
                w8("g", g, "pe", pe_q(eb - 2))
            g.dma_start(out=wtile[par][:, :], in_=wq_d.ap()[eb, :, :]
                        ).then_inc(SEM[f"w{par}"], 16)
            C[f"w{par}"] += 16
        g.dma_start(out=GSLOT[2][:, :], in_=wg_d.ap()[0, :, :]).then_inc(SEM["w2"], 16)
        g.dma_start(out=USLOT[2][:, :], in_=wu_d.ap()[0, :, :]).then_inc(SEM["w2"], 16)
        C["w2"] += 32
        w_tgt[0] = C["w2"]
        for k in range(8):
            g.dma_start(out=EXG[k][:, :], in_=wg_d.ap()[k + 1, :, :]
                        ).then_inc(SEM[f"e{k}"], 16)
            g.dma_start(out=EXU[k][:, :], in_=wu_d.ap()[k + 1, :, :]
                        ).then_inc(SEM[f"e{k}"], 16)
            C[f"e{k}"] += 32

    with nc.psum_tensor("p_pr0", [128, T], F32) as p_pr0, \
         nc.psum_tensor("p_pr1", [128, T], F32) as p_pr1:
        p_pr = [p_pr0, p_pr1]

        @blk.tensor
        def _(t):
            w8("t", t, "dv", dv0)
            for eb in range(EB):
                par = eb % 2
                w8("t", t, f"w{par}", w_base[par] + 16 * (eb // 2 + 1))
                if eb >= 2:
                    w8("t", t, "ac", ac_k(eb - 2))
                ins = None
                for dc in range(DC):
                    for j in range(2):
                        ins = t.matmul(
                            p_pr[par][:, j * 512:(j + 1) * 512],
                            wtile[par][:, dc * 128:(dc + 1) * 128],
                            xnT[:, dc * T + j * 512: dc * T + (j + 1) * 512],
                            start=(dc == 0), stop=(dc == DC - 1))
                ins.then_inc(SEM["pe"], 1)

        @blk.scalar
        def _(sc):
            w8("s", sc, "dv", dv_bias_end)
            for eb in range(EB):
                par = eb % 2
                w8("s", sc, "pe", pe_k(eb))
                if eb >= 2:
                    w8("s", sc, f"st{par}", st_base[par] + 16 * (eb // 2))
                sc.activation(stg[par][:, :], p_pr[par][:, :], AF.Identity,
                              bias=bkt[:, eb:eb + 1]).then_inc(SEM["ac"], 1)

        @blk.sync
        def _(s):
            for eb in range(EB):
                par = eb % 2
                w8("sy", s, "ac", ac_k(eb))
                s.dma_start(out=kvi_d.ap()[eb * 128:(eb + 1) * 128, :],
                            in_=stg[par][:, :]).then_inc(SEM[f"st{par}"], 16)
                C[f"st{par}"] += 16

        with nc.psum_tensor("p_vA", [128, 2048], F32) as p_vA:
            def v_chunk(tb, ch):
                if tb % 2 == 0:
                    return p_vA[:, ch * 512:(ch + 1) * 512]
                if ch < 2:
                    return p_pr0[:, ch * 512:(ch + 1) * 512]
                return p_pr1[:, (ch - 2) * 512:(ch - 1) * 512]

            def v_half(tb, eh):
                if tb % 2 == 0:
                    return p_vA[:, eh * 1024:(eh + 1) * 1024]
                return (p_pr0 if eh == 0 else p_pr1)[:, :]

            @blk.tensor
            def _(t):
                w8("t", t, "wv", 16 * 17)
                for tb in range(TB):
                    if tb % 2 == 1:
                        w8("t", t, "ac", ac_k(EB - 1))
                    if tb >= 2:
                        w8("t", t, "ac", ac_v(tb - 2, 1))
                    ins = None
                    for dc in range(DC):
                        for ch in range(4):
                            ins = t.matmul(
                                v_chunk(tb, ch),
                                xnT[:, dc * T + tb * 128: dc * T + (tb + 1) * 128],
                                wvres[:, dc * 2048 + ch * 512: dc * 2048 + (ch + 1) * 512],
                                start=(dc == 0), stop=False)
                    for ch in range(4):
                        ins = t.matmul(v_chunk(tb, ch), ones128[0:1, :],
                                       bv_row[:, ch * 512:(ch + 1) * 512],
                                       start=False, stop=True)
                    ins.then_inc(SEM["pe"], 1)

            @blk.scalar
            def _(sc):
                for tb in range(TB):
                    w8("s", sc, "pe", pe_v(tb))
                    for eh in range(2):
                        w8("s", sc, f"st{eh}", st_base[eh] + 128 + 16 * tb)
                        sc.activation(stg[eh][:, :], v_half(tb, eh), AF.Identity
                                      ).then_inc(SEM["ac"], 1)

            @blk.sync
            def _(s):
                for tb in range(TB):
                    for eh in range(2):
                        w8("sy", s, "ac", ac_v(tb, eh))
                        dest = kvi_v[tb * 256:(tb + 1) * 256, :].rearrange(
                            "(p a) q -> p a q", a=2)[:, eh:eh + 1, :]
                        s.dma_start(out=dest,
                                    in_=stg[eh][:, :].rearrange("p (a q) -> p a q", a=1)
                                    ).then_inc(SEM[f"st{eh}"], 16)
                        C[f"st{eh}"] += 16

        @blk.tensor
        def _(t):
            w8("t", t, "ac", ac_v(TB - 1, 1))
            for eb in range(EB):
                par = eb % 2
                w8("t", t, f"w{par}", w_base[par] + 16 * (8 + eb // 2 + 1))
                if eb >= 2:
                    w8("t", t, "ac", ac_q(eb - 2))
                ins = None
                for dc in range(DC):
                    for j in range(2):
                        ins = t.matmul(
                            p_pr[par][:, j * 512:(j + 1) * 512],
                            wtile[par][:, dc * 128:(dc + 1) * 128],
                            xnT[:, dc * T + j * 512: dc * T + (j + 1) * 512],
                            start=(dc == 0), stop=(dc == DC - 1))
                ins.then_inc(SEM["pe"], 1)

        @blk.scalar
        def _(sc):
            for eb in range(EB):
                par = eb % 2
                w8("s", sc, "pe", pe_q(eb))
                if eb < 2:
                    w8("s", sc, f"st{par}", st_base[par] + 128 + 128)
                else:
                    w8("s", sc, f"qw{par}", 16 * (eb // 2))
                sc.activation(stg[par][:, :], p_pr[par][:, :], AF.Identity,
                              bias=bqt[:, eb:eb + 1]).then_inc(SEM["ac"], 1)

        @blk.sync
        def _(s):
            for eb in range(EB):
                par = eb % 2
                w8("sy", s, "ac", ac_q(eb))
                s.dma_start(out=qT_d.ap()[eb * 128:(eb + 1) * 128, :],
                            in_=stg[par][:, :]).then_inc(SEM[f"qw{par}"], 16)
                C[f"qw{par}"] += 16
    C["pe"] += 40
    C["ac"] += 48
    pe_p3_vend = C["pe"] - 16
    pe_p3_end = C["pe"]

    sg = [sg0, sg1]
    with nc.psum_tensor("p_g0", [128, T], F32) as p_g0, \
         nc.psum_tensor("p_g1", [128, T], F32) as p_g1, \
         nc.psum_tensor("p_u0", [128, T], F32) as p_u0, \
         nc.psum_tensor("p_u1", [128, T], F32) as p_u1:
        p_g = [p_g0, p_g1]
        p_u = [p_u0, p_u1]
        pe0, ac0, dv0 = C["pe"], C["ac"], C["dv"]
        st_final = {0: C["st0"], 1: C["st1"]}

        @blk.gpsimd
        def _(g):
            w8("g", g, "st0", st_final[0])
            w8("g", g, "st1", st_final[1])
            g.collective_compute(
                "AllGather", ALU.bypass,
                replica_groups=[[0, 1, 2, 3], [4, 5, 6, 7]],
                ins=[kvi_d.ap().opt()], outs=[kvo_d.ap().opt()],
            ).then_inc(SEM["cc"], 1)
            C["cc"] = 1
            w8("g", g, "pe", pe_p3_vend)
            for k in range(4):
                g.dma_start(out=EXG2[k][:, :], in_=wg_d.ap()[k + 9, :, :]
                            ).then_inc(SEM[f"f{k}"], 16)
                g.dma_start(out=EXU2[k][:, :], in_=wu_d.ap()[k + 9, :, :]
                            ).then_inc(SEM[f"f{k}"], 16)
                C[f"f{k}"] += 32
            for ib in range(13, IB):
                s = gslot(ib)
                if ib >= 16:
                    w8("g", g, "pe", pe0 + 2 * (ib - 3) + 2)
                elif ib == 15:
                    w8("g", g, "pe", pe0 + 2)
                else:
                    w8("g", g, "pe", pe0)
                g.dma_start(out=GSLOT[s][:, :], in_=wg_d.ap()[ib, :, :]
                            ).then_inc(SEM[f"w{s}"], 16)
                g.dma_start(out=USLOT[s][:, :], in_=wu_d.ap()[ib, :, :]
                            ).then_inc(SEM[f"w{s}"], 16)
                C[f"w{s}"] += 32
                w_tgt[ib] = C[f"w{s}"]

        @blk.tensor
        def _(t):
            w8("t", t, "dv", dv0)
            w8("t", t, "ac", ac0)
            for ib in range(IB):
                par = ib % 2
                if 1 <= ib <= 8:
                    gtile, utile = EXG[ib - 1], EXU[ib - 1]
                    w8("t", t, f"e{ib - 1}", 32)
                elif 9 <= ib <= 12:
                    gtile, utile = EXG2[ib - 9], EXU2[ib - 9]
                    w8("t", t, f"f{ib - 9}", 32)
                else:
                    s = gslot(ib)
                    gtile, utile = GSLOT[s], USLOT[s]
                    w8("t", t, f"w{s}", w_tgt[ib])
                if ib >= 2:
                    w8("t", t, "ac", ac0 + ib - 1)
                    w8("t", t, "dv", dv0 + 3 * (ib - 2) + 3)
                ins = None
                for dc in range(DC):
                    for j in range(2):
                        ins = t.matmul(p_g[par][:, j * 512:(j + 1) * 512],
                                       gtile[:, dc * 128:(dc + 1) * 128],
                                       xnT[:, dc * T + j * 512: dc * T + (j + 1) * 512],
                                       start=(dc == 0), stop=(dc == DC - 1))
                ins.then_inc(SEM["pe"], 1)
                for dc in range(DC):
                    for j in range(2):
                        ins = t.matmul(p_u[par][:, j * 512:(j + 1) * 512],
                                       utile[:, dc * 128:(dc + 1) * 128],
                                       xnT[:, dc * T + j * 512: dc * T + (j + 1) * 512],
                                       start=(dc == 0), stop=(dc == DC - 1))
                ins.then_inc(SEM["pe"], 1)

        @blk.scalar
        def _(sc):
            for ib in range(IB):
                par = ib % 2
                w8("s", sc, "dv", dv0 + 3 * ib + 1)
                if ib >= 2:
                    w8("s", sc, "dv", dv0 + 3 * (ib - 2) + 3)
                sc.activation(sg[par][:, :], p_g[par][:, :], AF.Silu,
                              scale=gst[:, ib:ib + 1]).then_inc(SEM["ac"], 1)

        @blk.vector
        def _(v):
            w8("v", v, "st0", C["st0"])
            w8("v", v, "st1", C["st1"])
            w8("v", v, "qw0", C["qw0"])
            w8("v", v, "qw1", C["qw1"])
            for ib in range(IB):
                par = ib % 2
                w8("v", v, "pe", pe0 + 2 * ib + 1)
                v.tensor_add(p_g[par][:, :], p_g[par][:, :], rsb128[:, :]
                             ).then_inc(SEM["dv"], 1)
                w8("v", v, "pe", pe0 + 2 * ib + 2)
                v.tensor_add(p_u[par][:, :], p_u[par][:, :], rsb128[:, :]
                             ).then_inc(SEM["dv"], 1)
                w8("v", v, "ac", ac0 + ib + 1)
                w8("v", v, "dv", dv0 + 3 * ib + 2)
                v.scalar_tensor_tensor(
                    out=h_sb[:, ib * T:(ib + 1) * T], in0=p_u[par][:, :],
                    scalar=ust[:, ib:ib + 1], in1=sg[par][:, :],
                    op0=ALU.mult, op1=ALU.mult).then_inc(SEM["dv"], 1)
        C["pe"] += 128
        C["ac"] += 64
        C["dv"] += 3 * IB
    pe_p5_end = C["pe"]

    with nc.psum_tensor("p_hs", [1, T], F32) as p_hs:
        pe0, dv0 = C["pe"], C["dv"]

        @blk.tensor
        def _(t):
            w8("t", t, "dv", dv0)
            ins = None
            for ib in range(IB):
                for j in range(2):
                    ins = t.matmul(p_hs[:, j * 512:(j + 1) * 512], ones128[:, 0:1],
                                   h_sb[:, ib * T + j * 512: ib * T + (j + 1) * 512],
                                   start=(ib == 0), stop=(ib == IB - 1))
            ins.then_inc(SEM["pe"], 1)

        with nc.psum_tensor("p_bc2", [128, T], F32) as p_bc2:
            @blk.vector
            def _(v):
                w8("v", v, "pe", pe0 + 1)
                v.tensor_copy(vrow_f[:, :], p_hs[:, :]).then_inc(SEM["dv"], 1)
                w8("v", v, "dv", dv0 + 1)
                v.tensor_copy(vrow_b[:, :], vrow_f[:, :]).then_inc(SEM["dv"], 1)
                w8("v", v, "pe", pe0 + 2)
                v.tensor_copy(rsb128[:, :], p_bc2[:, :]).then_inc(SEM["dv"], 1)

            @blk.tensor
            def _(t):
                w8("t", t, "dv", dv0 + 2)
                ins = None
                for j in range(2):
                    ins = t.matmul(p_bc2[:, j * 512:(j + 1) * 512], negrow[:, :],
                                   vrow_b[:, j * 512:(j + 1) * 512],
                                   start=True, stop=True)
                ins.then_inc(SEM["pe"], 1)
        C["pe"] += 2
        C["dv"] += 3

    fst = [sg0, sg1]
    with nc.psum_tensor("p_d0", [128, T], F32) as p_d0, \
         nc.psum_tensor("p_d1", [128, T], F32) as p_d1:
        p_d = [p_d0, p_d1]
        pe0, ac0, dv0 = C["pe"], C["ac"], C["dv"]
        w_base = {k: C[f"w{k}"] for k in range(3)}
        fw_base = {0: C["fw0"], 1: C["fw1"]}
        pe_eb = lambda eb: pe0 + eb + 1

        @blk.gpsimd
        def _(g):
            for hj in range(2 * EB):
                eb, hf = divmod(hj, 2)
                sl = hj % 3
                if hj >= 3:
                    w8("g", g, "pe", pe_eb((hj - 3) // 2))
                else:
                    w8("g", g, "pe", pe_p5_end)
                g.dma_start(out=WD[sl][:, :],
                            in_=wd_d.ap()[eb, :, hf * 4096:(hf + 1) * 4096]
                            ).then_inc(SEM[f"w{sl}"], 16)
                C[f"w{sl}"] += 16

        @blk.tensor
        def _(t):
            w8("t", t, "dv", dv0)
            for eb in range(EB):
                par = eb % 2
                if eb >= 2:
                    w8("t", t, "ac", ac0 + eb - 1)
                ins = None
                for hf in range(2):
                    hj = 2 * eb + hf
                    sl = hj % 3
                    w8("t", t, f"w{sl}", w_base[sl] + 16 * (hj // 3 + 1))
                    for dc32 in range(32):
                        dc = hf * 32 + dc32
                        for j in range(2):
                            ins = t.matmul(p_d[par][:, j * 512:(j + 1) * 512],
                                           WD[sl][:, dc32 * 128:(dc32 + 1) * 128],
                                           h_sb[:, dc * T + j * 512: dc * T + (j + 1) * 512],
                                           start=(dc == 0), stop=(dc == IB - 1))
                ins.then_inc(SEM["pe"], 1)

        @blk.vector
        def _(v):
            for eb in range(EB):
                par = eb % 2
                w8("v", v, "pe", pe_eb(eb))
                if eb >= 2:
                    w8("v", v, "ac", ac0 + eb - 1)
                v.tensor_add(p_d[par][:, :], p_d[par][:, :], rsb128[:, :]
                             ).then_inc(SEM["dv"], 1)

        @blk.scalar
        def _(sc):
            for eb in range(EB):
                par = eb % 2
                w8("s", sc, "dv", dv0 + eb + 1)
                if eb >= 2:
                    w8("s", sc, f"fw{par}", fw_base[par] + 16 * (eb // 2))
                sc.activation(fst[par][:, :], p_d[par][:, :], AF.Identity,
                              scale=dst[:, eb:eb + 1], bias=bot[:, eb:eb + 1]
                              ).then_inc(SEM["ac"], 1)
                w8("s", sc, "ac", ac0 + eb + 1)
                sc.dma_start(out=ffn_d.ap()[eb * 128:(eb + 1) * 128, :],
                             in_=fst[par][:, :]).then_inc(SEM[f"fw{par}"], 16)
                C[f"fw{par}"] += 16
        C["pe"] += 16
        C["ac"] += 16
        C["dv"] += EB
    pe_p7_end = C["pe"]
    fw_end = {0: C["fw0"], 1: C["fw1"]}
    qw_end = {0: C["qw0"], 1: C["qw1"]}

    kh = [kh0, kh1]
    vh = [vh0, vh1]
    qh = [qh0, qh1]
    pT = [pT0, pT1]
    GROUPS = [(0, 3), (3, 6), (6, 9), (9, 12), (12, 15), (15, 18), (18, 21),
              (21, 24), (24, 27), (27, 30), (30, 32)]
    NG = len(GROUPS)
    NIDX = 2 * NH
    SLICES = [(0, 6), (6, 12), (12, 18), (18, 24), (24, 30), (30, 36), (36, 42),
              (42, 48), (48, 54), (54, 60), (60, 64)]

    pe0 = C["pe"]
    pe_cnt = pe0
    score_tgt = {}
    den_tgt = {}
    av_tgt = {}
    plan = []
    for idx in range(NIDX + 1):
        for g in range(NG):
            if idx < NIDX:
                pe_cnt += 1
                score_tgt[(idx, g)] = pe_cnt
                plan.append(("score", idx, g))
            if idx >= 1:
                pe_cnt += 1
                s0, s1 = SLICES[g]
                plan.append(("denav", idx - 1, g))
                if s0 <= 31 < s1 or s1 == 32:
                    pass
                if s1 >= 32 and s0 < 32:
                    den_tgt[idx - 1] = pe_cnt
                if s1 == 64:
                    av_tgt[idx - 1] = pe_cnt
    assert len(den_tgt) == NIDX and len(av_tgt) == NIDX
    pe_p8_end = pe_cnt

    ac0, dv0 = C["ac"], C["dv"]
    ac_exp = lambda idx, g: ac0 + idx * NG + g + 1
    ac_exp_end = lambda idx: ac0 + (idx + 1) * NG
    dv_rec = lambda idx: dv0 + 2 * idx + 1
    dv_omul = lambda idx: dv0 + 2 * idx + 2

    with nc.psum_tensor("p_sA", [128, 1536], F32) as p_sA, \
         nc.psum_tensor("p_sB", [128, 1536], F32) as p_sB, \
         nc.psum_tensor("p_den", [128, 512], F32) as p_den, \
         nc.psum_tensor("p_o", [128, 512], F32) as p_o:
        p_s = [p_sA, p_sB]
        kh_base = {0: C["kh0"], 1: C["kh1"]}
        vh_base = {0: C["vh0"], 1: C["vh1"]}
        qh_base = {0: C["qh0"], 1: C["qh1"]}

        @blk.sync
        def _(s):
            w8("sy", s, "cc", 1)
            w8("sy", s, "qw0", qw_end[0])
            w8("sy", s, "qw1", qw_end[1])
            for h in range(NH):
                par = h % 2
                if h >= 2:
                    w8("sy", s, "pe", av_tgt[2 * (h - 2) + 1])
                else:
                    w8("sy", s, "pe", pe_p5_end)
                for r in range(4):
                    s.dma_start(out=kh[par][:, r * T:(r + 1) * T],
                                in_=kvo_d.ap()[r * 4096 + h * 128: r * 4096 + (h + 1) * 128, :]
                                ).then_inc(SEM[f"kh{par}"], 16)
                    C[f"kh{par}"] += 16
                for r in range(4):
                    src_v = kvo_d.ap()[r * 4096 + D: r * 4096 + 2 * D,
                                       (h % 8) * 128:(h % 8 + 1) * 128].rearrange(
                        "(tt p a) e -> p tt a e", tt=8, p=128)[:, :, h // 8: h // 8 + 1, :]
                    s.dma_start(out=vh[par][:, r * T:(r + 1) * T].rearrange(
                                    "p (tt a e) -> p tt a e", tt=8, a=1),
                                in_=src_v).then_inc(SEM[f"vh{par}"], 16)
                    C[f"vh{par}"] += 16
                s.dma_start(out=qh[par][:, :], in_=qT_d.ap()[h * 128:(h + 1) * 128, :]
                            ).then_inc(SEM[f"qh{par}"], 16)
                C[f"qh{par}"] += 16

        @blk.tensor
        def _(t):
            w8("t", t, "ac", ac0)
            for kind, idx, g in plan:
                h, j = divmod(idx, 2)
                par = h % 2
                if kind == "score":
                    gg = idx * NG + g
                    if g == 0 and j == 0:
                        w8("t", t, f"kh{par}", kh_base[par] + 64 * (h // 2 + 1))
                        w8("t", t, f"qh{par}", qh_base[par] + 16 * (h // 2 + 1))
                    if gg >= 2:
                        w8("t", t, "ac", ac_exp(*divmod(gg - 2, NG)))
                    b0, b1 = GROUPS[g]
                    ins = None
                    for b in range(b0, b1):
                        ins = t.matmul(p_s[gg % 2][:, (b - b0) * 512:(b - b0 + 1) * 512],
                                       kh[par][:, b * 128:(b + 1) * 128],
                                       qh[par][:, j * 512:(j + 1) * 512],
                                       start=True, stop=True)
                    ins.then_inc(SEM["pe"], 1)
                else:
                    s0, s1 = SLICES[g]
                    if g == 0:
                        w8("t", t, "ac", ac_exp_end(idx))
                        w8("t", t, f"vh{par}", vh_base[par] + 64 * (h // 2 + 1))
                    ins = None
                    for e in range(s0, s1):
                        if e == 0 and idx >= 1:
                            w8("t", t, "dv", dv_rec(idx - 1))
                        if e == 32 and idx >= 1:
                            w8("t", t, "dv", dv_omul(idx - 1))
                        if e < 32:
                            ins = t.matmul(p_den[:, :], ones128[:, :],
                                           pT[idx % 2][:, e * 512:(e + 1) * 512],
                                           start=(e == 0), stop=(e == 31))
                        else:
                            b = e - 32
                            ins = t.matmul(p_o[:, :], vh[par][:, b * 128:(b + 1) * 128],
                                           pT[idx % 2][:, b * 512:(b + 1) * 512],
                                           start=(b == 0), stop=(b == 31))
                    ins.then_inc(SEM["pe"], 1)

        @blk.scalar
        def _(sc):
            for idx in range(NIDX):
                if idx == NIDX - 1:
                    w8("s", sc, "fw0", fw_end[0])
                    w8("s", sc, "fw1", fw_end[1])
                    for eb in range(2):
                        sc.dma_start(out=(fti0 if eb == 0 else fti1)[:, :],
                                     in_=ffn_d.ap()[eb * 128:(eb + 1) * 128, :]
                                     ).then_inc(SEM[f"fr{eb % 2}"], 16)
                        C[f"fr{eb % 2}"] += 16
                for g, (b0, b1) in enumerate(GROUPS):
                    w8("s", sc, "pe", score_tgt[(idx, g)])
                    if g == 0 and idx >= 2:
                        w8("s", sc, "pe", av_tgt[idx - 2])
                    sc.activation(pT[idx % 2][:, b0 * 512:b1 * 512],
                                  p_s[(idx * NG + g) % 2][:, 0:(b1 - b0) * 512], AF.Exp,
                                  scale=SC).then_inc(SEM["ac"], 1)
            w8("s", sc, "pe", av_tgt[NIDX - 2])
            for eb in range(8):
                sc.dma_start(
                    out=xf1[:, eb * T:(eb + 1) * T].rearrange(
                        "p (tb e) -> p tb e", e=128),
                    in_=x_d.ap()[:, eb * 128:(eb + 1) * 128].rearrange(
                        "(tb p) e -> p tb e", p=128)).then_inc(SEM["xf"], 16)
                C["xf"] += 16

        @blk.vector
        def _(v):
            for idx in range(NIDX):
                h, j = divmod(idx, 2)
                w8("v", v, "pe", den_tgt[idx])
                if idx >= 1:
                    w8("v", v, "dv", dv_omul(idx - 1))
                v.reciprocal(recipb[:, :], p_den[:, :]).then_inc(SEM["dv"], 1)
                w8("v", v, "pe", av_tgt[idx])
                w8("v", v, "dv", dv_rec(idx))
                v.tensor_mul(oT[:, h * T + j * 512: h * T + (j + 1) * 512],
                             p_o[:, :], recipb[:, :]).then_inc(SEM["dv"], 1)
        C["pe"] = pe_p8_end
        C["ac"] += NIDX * NG
        C["dv"] += NIDX * 2

    fti = [fti0, fti1]
    ote = [ote0, ote1]
    with nc.psum_tensor("p_w0", [128, T], F32) as p_w0, \
         nc.psum_tensor("p_w1", [128, T], F32) as p_w1, \
         nc.psum_tensor("p_f0", [128, 128], F32) as p_f0, \
         nc.psum_tensor("p_f1", [128, 128], F32) as p_f1, \
         nc.psum_tensor("p_f2", [128, 128], F32) as p_f2, \
         nc.psum_tensor("p_f3", [128, 128], F32) as p_f3:
        p_w = [p_w0, p_w1]
        p_f = [p_f0, p_f1, p_f2, p_f3]
        pe0, ac0, dv0 = C["pe"], C["ac"], C["dv"]
        w_base = {0: C["w0"], 1: C["w1"]}
        ou_base = {i: C[f"ou{i}"] for i in range(8)}
        pe_proj = {}
        pe_tp = {}
        cnt = pe0
        for eb in range(EB):
            cnt += 1
            pe_proj[eb] = cnt
            if eb >= 1:
                for tb in range(TB):
                    cnt += 1
                    pe_tp[(eb - 1, tb)] = cnt
        for tb in range(TB):
            cnt += 1
            pe_tp[(EB - 1, tb)] = cnt
        pe_p9_end = cnt
        dv_ote = lambda eb: dv0 + eb * 9 + 1
        dv_of = lambda eb, tb: dv0 + eb * 9 + 1 + tb + 1

        @blk.gpsimd
        def _(g):
            for eb in range(EB):
                par = eb % 2
                if eb >= 2:
                    w8("g", g, "pe", pe_proj[eb - 2])
                else:
                    w8("g", g, "pe", pe_p7_end)
                g.dma_start(out=wtile[par][:, :], in_=wo_d.ap()[eb, :, :]
                            ).then_inc(SEM[f"w{par}"], 16)
                C[f"w{par}"] += 16

        @blk.scalar
        def _(sc):
            w8("s", sc, "pe", pe_p8_end)
            for eb in range(2, EB):
                par = eb % 2
                w8("s", sc, "dv", dv_ote(eb - 2))
                sc.dma_start(out=fti[par][:, :],
                             in_=ffn_d.ap()[eb * 128:(eb + 1) * 128, :]
                             ).then_inc(SEM[f"fr{par}"], 16)
                C[f"fr{par}"] += 16
                if eb == 2:
                    for eb2 in range(8, EB):
                        sc.dma_start(
                            out=xf2[:, (eb2 - 8) * T:(eb2 - 7) * T].rearrange(
                                "p (tb e) -> p tb e", e=128),
                            in_=x_d.ap()[:, eb2 * 128:(eb2 + 1) * 128].rearrange(
                                "(tb p) e -> p tb e", p=128)).then_inc(SEM["xf2"], 16)
                        C["xf2"] += 16

        @blk.tensor
        def _(t):
            w8("t", t, "dv", dv0)
            def proj(eb):
                par = eb % 2
                w8("t", t, f"w{par}", w_base[par] + 16 * (eb // 2 + 1))
                if eb >= 2:
                    w8("t", t, "dv", dv_ote(eb - 2))
                ins = None
                for dc in range(DC):
                    for j in range(2):
                        ins = t.matmul(p_w[par][:, j * 512:(j + 1) * 512],
                                       wtile[par][:, dc * 128:(dc + 1) * 128],
                                       oT[:, dc * T + j * 512: dc * T + (j + 1) * 512],
                                       start=(dc == 0), stop=(dc == DC - 1))
                ins.then_inc(SEM["pe"], 1)

            def transp(eb):
                par = eb % 2
                w8("t", t, "dv", dv_ote(eb))
                for tb in range(TB):
                    k = eb * TB + tb
                    if k >= 4:
                        pk_eb, pk_tb = divmod(k - 4, TB)
                        w8("t", t, "dv", dv_of(pk_eb, pk_tb))
                    t.transpose(p_f[tb % 4][:, :], ote[par][:, tb * 128:(tb + 1) * 128],
                                ident_f[:, :]).then_inc(SEM["pe"], 1)

            for eb in range(EB):
                proj(eb)
                if eb >= 1:
                    transp(eb - 1)
            transp(EB - 1)

        @blk.vector
        def _(v):
            for eb in range(EB):
                par = eb % 2
                w8("v", v, "pe", pe_proj[eb])
                w8("v", v, f"fr{par}", 16 * (eb // 2 + 1))
                if eb >= 2:
                    w8("v", v, "pe", pe_tp[(eb - 2, TB - 1)])
                v.tensor_add(ote[par][:, :], p_w[par][:, :], fti[par][:, :]
                             ).then_inc(SEM["dv"], 1)
                w8("v", v, "xf" if eb < 8 else "xf2", 128)
                xsl = (xf1[:, eb * T + 0:eb * T + T] if eb < 8
                       else xf2[:, (eb - 8) * T:(eb - 7) * T])
                for tb in range(TB):
                    k = eb * TB + tb
                    w8("v", v, "pe", pe_tp[(eb, tb)])
                    if k >= 8:
                        pk = k - 8
                        w8("v", v, f"ou{pk % 8}", ou_base[pk % 8] + 16 * (pk // 8 + 1))
                    v.tensor_add(OF[k % 8][:, :], p_f[tb % 4][:, :],
                                 xsl[:, tb * 128:(tb + 1) * 128]).then_inc(SEM["dv"], 1)

        @blk.sync
        def _(s):
            for eb in range(EB):
                for tb in range(TB):
                    k = eb * TB + tb
                    w8("sy", s, "dv", dv_of(eb, tb))
                    s.dma_start(out=out_d.ap()[tb * 128:(tb + 1) * 128,
                                               eb * 128:(eb + 1) * 128],
                                in_=OF[k % 8][:, :]).then_inc(SEM[f"ou{k % 8}"], 16)
                    C[f"ou{k % 8}"] += 16
            for i in range(8):
                w8("sy", s, f"ou{i}", ou_base[i] + 16 * 16)
        C["pe"] = pe_p9_end
        C["dv"] += EB * 9

    es.close()
    return nc


_nc_cache = None


def _get_nc():
    global _nc_cache
    if _nc_cache is None:
        _nc_cache = build_nc()
    return _nc_cache


def _block_T(w, nb_out, nb_in):
    wT = np.ascontiguousarray(w.T)
    a = wT.reshape(nb_in, 128, nb_out, 128)
    return np.ascontiguousarray(a.transpose(2, 1, 0, 3).reshape(nb_out, 128, nb_in * 128))


def kernel(**inp):
    x = np.ascontiguousarray(np.asarray(inp["L0_attn_x"], dtype=np.float32))
    wq = np.asarray(inp["L0_attn_wq"], np.float32)
    wk = np.asarray(inp["L0_attn_wk"], np.float32)
    wv = np.asarray(inp["L0_attn_wv"], np.float32)
    wo = np.asarray(inp["L0_attn_wo"], np.float32)
    bq = np.ascontiguousarray(np.asarray(inp["L0_attn_bq"], np.float32))
    bk = np.ascontiguousarray(np.asarray(inp["L0_attn_bk"], np.float32))
    bv = np.ascontiguousarray(np.asarray(inp["L0_attn_bv"], np.float32))
    bo = np.ascontiguousarray(np.asarray(inp["L0_attn_bo"], np.float32))
    gw = np.asarray(inp["L0_ffn_gate_w"], np.int32)
    uw = np.asarray(inp["L0_ffn_up_w"], np.int32)
    dw = np.asarray(inp["L0_ffn_down_w"], np.int32)
    gs = np.ascontiguousarray(np.asarray(inp["L0_ffn_gate_scale"], np.float32))
    us = np.ascontiguousarray(np.asarray(inp["L0_ffn_up_scale"], np.float32))
    dsc = np.ascontiguousarray(np.asarray(inp["L0_ffn_down_scale"], np.float32))

    shared = dict(
        wq=_block_T(wq, EB, DC), wk=_block_T(wk, EB, DC),
        wv=np.ascontiguousarray(wv.T.reshape(DC, 128, D)),
        wo=_block_T(wo, EB, DC),
        bq=bq, bk=bk, bv=bv, bo=bo,
        wg=_block_T(gw, IB, DC), wu=_block_T(uw, IB, DC),
        wd=_block_T(dw, EB, IB),
        gs=gs, us=us, dsc=dsc,
        ident=np.eye(128, dtype=np.float32),
    )
    xs = x.reshape(8, T, D)
    in_maps = [dict(shared, x=np.ascontiguousarray(xs[c])) for c in range(8)]

    nc = _get_nc()
    res = run_bass_kernel_spmd(nc, in_maps, core_ids=list(range(8)))
    outs = [np.asarray(res.results[c]["out"]) for c in range(8)]
    return np.concatenate(outs, axis=0).reshape(2, 4096, 2048)
